# version 03 (final)
# baseline (speedup 1.0000x reference)
import contextlib
import math
import numpy as np

import concourse.bass as bass
import concourse.mybir as mybir
from concourse.bass_utils import run_bass_kernel_spmd

F32 = mybir.dt.float32
BF16 = mybir.dt.bfloat16
I32 = mybir.dt.int32
AF = mybir.ActivationFunctionType
ALU = mybir.AluOpType
AX = mybir.AxisListType

ENGINES = ("tensor", "vector", "scalar", "gpsimd", "sync")


def _ap_box(ap):
    if not hasattr(ap, "tensor"):
        ap = ap[:]
    t = ap.tensor
    name = t.name
    lo = ap.offset
    hi = ap.offset
    for step, count in ap.ap:
        if count > 1:
            if step >= 0:
                hi += step * (count - 1)
            else:
                lo += step * (count - 1)
    return (name, lo, hi + 1)


def _overlap(b1, b2):
    return b1[0] == b2[0] and b1[1] < b2[2] and b2[1] < b1[2]


class MK:

    def __init__(self, nc):
        self.nc = nc
        self.items = {e: [] for e in ENGINES}
        self.counts = {e: 0 for e in ENGINES}
        self.dcounts = {}
        self.chan_eng = {}
        self.writers = {}
        self.readers = {}
        self.chan_maxwait = {}

    def _deps_for(self, reads, writes):
        deps = []
        for ap in reads:
            b = _ap_box(ap)
            for (wb, tok) in self.writers.get(b[0], ()):
                if _overlap(b, wb):
                    deps.extend(tok)
        for ap in writes:
            b = _ap_box(ap)
            for (wb, tok) in self.writers.get(b[0], ()):
                if _overlap(b, wb):
                    deps.extend(tok)
            for (rb, tok) in self.readers.get(b[0], ()):
                if _overlap(b, rb):
                    deps.extend(tok)
        return deps

    def _record(self, reads, writes, tok):
        for ap in reads:
            b = _ap_box(ap)
            lst = self.readers.setdefault(b[0], [])
            lst.append((b, tok))
            if len(lst) > 48:
                name = b[0]
                lo = min(x[0][1] for x in lst); hi = max(x[0][2] for x in lst)
                self.readers[name] = [((name, lo, hi), self._merge_toks(lst))]
        for ap in writes:
            b = _ap_box(ap)
            name = b[0]
            lst = self.writers.setdefault(name, [])
            lst = [(wb, t) for (wb, t) in lst if not (b[1] <= wb[1] and wb[2] <= b[2])]
            lst.append((b, tok))
            if len(lst) > 48:
                lo = min(x[0][1] for x in lst); hi = max(x[0][2] for x in lst)
                lst = [((name, lo, hi), self._merge_toks(lst))]
            self.writers[name] = lst
            rl = self.readers.get(name)
            if rl:
                self.readers[name] = [(rb, t) for (rb, t) in rl if not (b[1] <= rb[1] and rb[2] <= b[2])]

    def _merge_toks(self, lst):
        out = {}
        for (_, tok) in lst:
            for (s, v) in tok:
                out[s] = max(out.get(s, 0), v)
        return list(out.items())

    def op(self, eng, fn, reads=(), writes=(), extra_deps=(), dma=False, chan=0):
        reads = list(reads); writes = list(writes)
        deps = self._deps_for(reads, writes) + [d for t in extra_deps for d in t]
        waits = {}
        for (s, v) in deps:
            if s.startswith("dq:"):
                k2 = s[3:]
                v = self.dcounts[k2]
                self.chan_maxwait[k2] = max(self.chan_maxwait.get(k2, 0), v)
                s = "d_" + k2
            if (not dma) and s == "c_" + eng == "c_tensor":
                continue
            waits[s] = max(waits.get(s, 0), v)
        if dma:
            key = f"{eng}_{chan}"
            self.chan_eng[key] = eng
            g = self.chan_maxwait.get(key, 0)
            if g > self.dcounts.get(key, 0):
                g = self.dcounts.get(key, 0)
            if g:
                waits["d_" + key] = max(waits.get("d_" + key, 0), g)
            self.dcounts[key] = self.dcounts.get(key, 0) + 16
            tok = [("dq:" + key, 0)]
            semname = "d_" + key
        else:
            self.counts[eng] += 1
            tok = [("c_" + eng, self.counts[eng])]
            semname = "c_" + eng
        self.items[eng].append((fn, waits, semname, 16 if dma else 1))
        self._record(reads, writes, tok)
        return tok

    def t(self, fn, reads=(), writes=(), extra_deps=()):
        return self.op("tensor", fn, reads, writes, extra_deps)

    def v(self, fn, reads=(), writes=(), extra_deps=()):
        return self.op("vector", fn, reads, writes, extra_deps)

    def s(self, fn, reads=(), writes=(), extra_deps=()):
        return self.op("scalar", fn, reads, writes, extra_deps)

    def g(self, fn, reads=(), writes=(), extra_deps=()):
        return self.op("gpsimd", fn, reads, writes, extra_deps)

    def dma(self, eng, out, in_, extra_deps=(), chan=0, **kw):
        return self.op(eng, lambda E: E.dma_start(out=out, in_=in_, **kw),
                       reads=[in_], writes=[out], extra_deps=extra_deps, dma=True, chan=chan)

    def barrier_tok(self):
        toks = [("c_" + e, self.counts[e]) for e in ENGINES if self.counts[e]]
        for key, v in self.dcounts.items():
            toks.append(("d_" + key, v))
        return toks

    def emit(self):
        nc = self.nc
        es = contextlib.ExitStack()
        sems = {}
        with es:
            for e in ENGINES:
                sems["c_" + e] = es.enter_context(nc.semaphore("sem_c_" + e))
            for key in self.dcounts:
                sems["d_" + key] = es.enter_context(nc.semaphore("sem_d_" + key))
            block = es.enter_context(nc.Block())

            def mk_section(eng):
                def body(E):
                    last = {}
                    for (fn, waits, semname, inc) in self.items[eng]:
                        for s, v in sorted(waits.items()):
                            if v <= 0 or last.get(s, -1) >= v:
                                continue
                            E.wait_ge(sems[s], v)
                            last[s] = v
                        inst = fn(E)
                        inst.then_inc(sems[semname], inc)
                return body

            for eng in ENGINES:
                if self.items[eng]:
                    getattr(block, eng)(mk_section(eng))
        return sems



def alloc(es, nc, name, shape, dtype=F32, psum=False):
    if psum:
        return es.enter_context(nc.psum_tensor(name, shape, dtype))
    return es.enter_context(nc.sbuf_tensor(name, shape, dtype))


def ceil_div(a, b):
    return (a + b - 1) // b



def u_bcast_row(mk, row_ap, out_ap, ones_ap, psum_ap, n, eng="vector"):
    for c0 in range(0, n, 512):
        c1 = min(n, c0 + 512)
        mk.t(lambda E, c0=c0, c1=c1: E.matmul(psum_ap[:, 0:c1 - c0], ones_ap, row_ap[:, c0:c1],
                                              start=True, stop=True),
             reads=[ones_ap, row_ap[:, c0:c1]], writes=[psum_ap[:, 0:c1 - c0]])
        mk.v(lambda E, c0=c0, c1=c1: E.tensor_copy(out_ap[:, c0:c1], psum_ap[:, 0:c1 - c0]),
             reads=[psum_ap[:, 0:c1 - c0]], writes=[out_ap[:, c0:c1]])


def u_ln(mk, x_ap, out_ap, stats_ap, D, eps_ap=None, sq_ap=None):
    sq = sq_ap if sq_ap is not None else out_ap
    mk.v(lambda E: E.reduce_sum(stats_ap[:, 0:1], x_ap, AX.X), reads=[x_ap], writes=[stats_ap[:, 0:1]])
    mk.v(lambda E: E.tensor_single_scalar(stats_ap[:, 0:1], stats_ap[:, 0:1], -1.0 / D, ALU.mult),
         reads=[stats_ap[:, 0:1]], writes=[stats_ap[:, 0:1]])
    mk.s(lambda E: E.activation(out=sq, in_=x_ap, func=AF.Square, bias=stats_ap[:, 0:1],
                                accum_out=stats_ap[:, 1:2]),
         reads=[x_ap, stats_ap[:, 0:1]], writes=[sq, stats_ap[:, 1:2]])
    mk.s(lambda E: E.activation(out=stats_ap[:, 2:3], in_=stats_ap[:, 1:2], func=AF.Ln,
                                scale=1.0 / D, bias=eps_ap),
         reads=[stats_ap[:, 1:2], eps_ap], writes=[stats_ap[:, 2:3]])
    mk.s(lambda E: E.activation(out=stats_ap[:, 2:3], in_=stats_ap[:, 2:3], func=AF.Exp, scale=-0.5),
         reads=[stats_ap[:, 2:3]], writes=[stats_ap[:, 2:3]])
    mk.v(lambda E: E.tensor_mul(stats_ap[:, 3:4], stats_ap[:, 0:1], stats_ap[:, 2:3]),
         reads=[stats_ap[:, 0:1], stats_ap[:, 2:3]], writes=[stats_ap[:, 3:4]])
    mk.v(lambda E: E.tensor_scalar(out_ap, x_ap, stats_ap[:, 2:3], stats_ap[:, 3:4],
                                   ALU.mult, ALU.add),
         reads=[x_ap, stats_ap[:, 2:4]], writes=[out_ap])


def u_transpose(mk, src_ap_fn, dst_ap_fn, nblk, identb_ap, psumT, rows=128):
    per_bank = 1024 // 128
    for i in range(nblk):
        slot = i % per_bank
        src = src_ap_fn(i)
        mk.t(lambda E, src=src, slot=slot: E.transpose(psumT[0:rows, slot * 128:(slot + 1) * 128],
                                                       src, identb_ap),
             reads=[src, identb_ap], writes=[psumT[0:rows, slot * 128:(slot + 1) * 128]])
        dst = dst_ap_fn(i)
        mk.v(lambda E, dst=dst, slot=slot: E.tensor_copy(dst, psumT[0:rows, slot * 128:(slot + 1) * 128]),
             reads=[psumT[0:rows, slot * 128:(slot + 1) * 128]], writes=[dst])


def u_mm(mk, psum_ap, lhsT_list, rhs_list):
    n = len(lhsT_list)
    for i, (l, r) in enumerate(zip(lhsT_list, rhs_list)):
        mk.t(lambda E, l=l, r=r, i=i: E.matmul(psum_ap, l, r, start=(i == 0), stop=(i == n - 1)),
             reads=[l, r], writes=[psum_ap])


def std_consts(nc):
    identb = nc.dram_tensor("c_identb", [128, 128], BF16, kind="ExternalInput").ap()
    onesf = nc.dram_tensor("c_ones", [1, 128], F32, kind="ExternalInput").ap()
    return identb, onesf


def std_consts_np():
    import ml_dtypes
    return {"c_identb": np.eye(128).astype(ml_dtypes.bfloat16),
            "c_ones": np.ones((1, 128), np.float32)}


def load_consts(mk, es, nc, identb_d, onesf_d):
    identb = alloc(es, nc, "s_identb", [128, 128], BF16)
    onesf = alloc(es, nc, "s_ones", [1, 128], F32)
    mk.dma("sync", identb[:], identb_d)
    mk.dma("sync", onesf[:], onesf_d)
    return identb, onesf



def build_quant():
    nc = bass.Bass("TRN2", target_bir_lowering=False)
    wfc = nc.dram_tensor("wfc", [4096, 1024], F32, kind="ExternalInput").ap()
    wproj = nc.dram_tensor("wproj", [1024, 4096], F32, kind="ExternalInput").ap()
    identb_d, onesf_d = std_consts(nc)
    o_fcT = nc.dram_tensor("o_fcT", [1024, 512], BF16, kind="ExternalOutput").ap()
    o_projT = nc.dram_tensor("o_projT", [4096, 128], BF16, kind="ExternalOutput").ap()
    wfc_my = nc.dram_tensor("wfc_my", [512, 1024], F32, kind="ExternalInput").ap()
    wproj_my = nc.dram_tensor("wproj_my", [128, 4096], F32, kind="ExternalInput").ap()

    mk = MK(nc)
    es = contextlib.ExitStack()
    with es:
        identb, onesf = load_consts(mk, es, nc, identb_d, onesf_d)
        tile = alloc(es, nc, "tile", [128, 4096])
        acc = alloc(es, nc, "acc", [128, 2])
        tsum = alloc(es, nc, "tsum", [128, 1])
        ones128c = alloc(es, nc, "ones128c", [128, 1])
        thr = {}
        ps = alloc(es, nc, "ps", [128, 512], psum=True)
        psT = alloc(es, nc, "psT", [128, 1024], BF16, psum=True)
        mk.v(lambda E: E.memset(ones128c[:], 1.0), writes=[ones128c[:]])

        for wname, w_ap, nrows, ncols in (("fc", wfc, 4096, 1024), ("proj", wproj, 1024, 4096)):
            mk.v(lambda E: E.memset(acc[:], 0.0), writes=[acc[:]])
            for i in range(nrows // 128):
                tap = tile[:, 0:ncols]
                mk.dma("sync", tap, w_ap[i * 128:(i + 1) * 128, :])
                mk.v(lambda E, tap=tap: E.tensor_reduce(tsum[:], tap, AX.X, ALU.add, True),
                     reads=[tap], writes=[tsum[:]])
                mk.v(lambda E: E.tensor_add(acc[:, 0:1], acc[:, 0:1], tsum[:]),
                     reads=[acc[:, 0:1], tsum[:]], writes=[acc[:, 0:1]])
            mk.t(lambda E: E.matmul(ps[0:1, 0:1], acc[:, 0:1], ones128c[:], start=True, stop=True),
                 reads=[acc[:, 0:1], ones128c[:]], writes=[ps[0:1, 0:1]])
            th = alloc(es, nc, f"thr_{wname}", [1, 1])
            mk.s(lambda E, th=th, nr=nrows, ncl=ncols: E.activation(
                    out=th[:], in_=ps[0:1, 0:1], func=AF.Copy, scale=0.5 / (nr * ncl), bias=0.5e-6),
                 reads=[ps[0:1, 0:1]], writes=[th[:]])
            th128 = alloc(es, nc, f"thr128_{wname}", [128, 1])
            mk.t(lambda E, th=th, th128=th128: E.matmul(ps[:, 0:1], onesf[:], th[:], start=True, stop=True),
                 reads=[onesf[:], th[:]], writes=[ps[:, 0:1]])
            mk.v(lambda E, th128=th128: E.tensor_copy(th128[:], ps[:, 0:1]),
                 reads=[ps[:, 0:1]], writes=[th128[:]])
            thr[wname] = th128

        for wname, wmy_ap, nrows, ncols, out_ap in (("fc", wfc_my, 512, 1024, o_fcT),
                                                    ("proj", wproj_my, 128, 4096, o_projT)):
            th128 = thr[wname]
            q = alloc(es, nc, f"q_{wname}", [128, ncols], BF16)
            ab = alloc(es, nc, f"ab_{wname}", [128, ncols])
            sg = alloc(es, nc, f"sg_{wname}", [128, ncols])
            outT = alloc(es, nc, f"outT_{wname}", [128, ncols], BF16)
            for i in range(nrows // 128):
                tap = tile[:, 0:ncols]
                mk.dma("sync", tap, wmy_ap[i * 128:(i + 1) * 128, :])
                mk.s(lambda E, tap=tap, ab=ab: E.activation(out=ab[:], in_=tap, func=AF.Abs),
                     reads=[tap], writes=[ab[:]])
                mk.s(lambda E, tap=tap, sg=sg: E.activation(out=sg[:], in_=tap, func=AF.Sign),
                     reads=[tap], writes=[sg[:]])
                mk.v(lambda E, ab=ab, th128=th128: E.tensor_scalar(ab[:], ab[:], th128[:], None, ALU.is_ge),
                     reads=[ab[:], th128[:]], writes=[ab[:]])
                mk.v(lambda E, q=q, ab=ab, sg=sg: E.tensor_mul(q[:], ab[:], sg[:]),
                     reads=[ab[:], sg[:]], writes=[q[:]])
                u_transpose(mk, lambda j, q=q: q[:, j * 128:(j + 1) * 128],
                            lambda j, outT=outT: outT[:, j * 128:(j + 1) * 128], ncols // 128, identb[:], psT)
                for j in range(ncols // 128):
                    mk.dma("gpsimd", out_ap[j * 128:(j + 1) * 128, i * 128:(i + 1) * 128],
                           outT[:, j * 128:(j + 1) * 128])
        mk.emit()
    return nc


def quant_inmaps(wfc, wproj, consts):
    ins = []
    for c in range(8):
        m = dict(consts)
        m["wfc"] = wfc
        m["wproj"] = wproj
        m["wfc_my"] = np.ascontiguousarray(wfc[c * 512:(c + 1) * 512])
        m["wproj_my"] = np.ascontiguousarray(wproj[c * 128:(c + 1) * 128])
        ins.append(m)
    return ins



def build_ffn(variant):
    nc = bass.Bass("TRN2", target_bir_lowering=False)
    xb_d = nc.dram_tensor("xb", [2048, 1024], F32, kind="ExternalInput").ap()
    has_p = variant in ("first", "post_s3")
    if has_p:
        p0_d = nc.dram_tensor("p0", [2048, 1024], F32, kind="ExternalInput").ap()
        p1_d = nc.dram_tensor("p1", [2048, 1024], F32, kind="ExternalInput").ap()
    if variant == "post_s3":
        brow_d = nc.dram_tensor("brow", [1, 1024], F32, kind="ExternalInput").ap()
    wfcT_d = nc.dram_tensor("wfcT", [1024, 4096], BF16, kind="ExternalInput").ap()
    wprojT_d = nc.dram_tensor("wprojT", [4096, 1024], BF16, kind="ExternalInput").ap()
    lnw_d = nc.dram_tensor("lnw", [1, 4096], F32, kind="ExternalInput").ap()
    lnb_d = nc.dram_tensor("lnb", [1, 4096], F32, kind="ExternalInput").ap()
    identb_d, onesf_d = std_consts(nc)
    xo_d = nc.dram_tensor("xo", [2048, 1024], F32, kind="ExternalOutput").ap()

    mk = MK(nc)
    es = contextlib.ExitStack()
    with es:
        identb, onesf = load_consts(mk, es, nc, identb_d, onesf_d)
        wfc = alloc(es, nc, "wfc", [128, 8 * 4096], BF16)
        wpr = alloc(es, nc, "wpr", [128, 32 * 1024], BF16)
        for k in range(8):
            mk.dma("sync", wfc[:, k * 4096:(k + 1) * 4096], wfcT_d[k * 128:(k + 1) * 128, :])
        for k in range(32):
            mk.dma("sync", wpr[:, k * 1024:(k + 1) * 1024], wprojT_d[k * 128:(k + 1) * 128, :])
        lnwr = alloc(es, nc, "lnwr", [1, 4096])
        lnbr = alloc(es, nc, "lnbr", [1, 4096])
        mk.dma("sync", lnwr[:], lnw_d)
        mk.dma("sync", lnbr[:], lnb_d)
        psb = alloc(es, nc, "psb", [128, 512], psum=True)
        lnw_bc = alloc(es, nc, "lnw_bc", [128, 4096], BF16)
        lnb_bc = alloc(es, nc, "lnb_bc", [128, 4096], BF16)
        u_bcast_row(mk, lnwr[:], lnw_bc[:], onesf[:], psb[:], 4096)
        u_bcast_row(mk, lnbr[:], lnb_bc[:], onesf[:], psb[:], 4096)
        if variant == "post_s3":
            brow_s = alloc(es, nc, "brow_s", [1, 1024])
            mk.dma("sync", brow_s[:], brow_d)
            brow_bc = alloc(es, nc, "brow_bc", [128, 1024], BF16)
            u_bcast_row(mk, brow_s[:], brow_bc[:], onesf[:], psb[:], 1024)

        eps128 = alloc(es, nc, "eps128", [128, 1])
        mk.v(lambda E: E.memset(eps128[:], 1e-5), writes=[eps128[:]])
        xt = alloc(es, nc, "xt", [128, 1024])
        pb = alloc(es, nc, "pb", [128, 1024])
        stats = alloc(es, nc, "stats", [128, 8])
        hbuf = alloc(es, nc, "hbuf", [128, 4096], BF16)
        hf = alloc(es, nc, "hf", [128, 4096], BF16)
        if variant == "post_s3":
            lnout = pb[:, 0:1024]
        normed = hbuf[:, 0:1024]
        xf = hf[:, 1024:2048]
        psT = alloc(es, nc, "psT", [128, 1024], BF16, psum=True)
        ph = alloc(es, nc, "ph", [128, 512], psum=True)
        ph2 = alloc(es, nc, "ph2", [128, 512], psum=True)
        po = alloc(es, nc, "po", [128, 512], psum=True)

        for i in range(16):
            r0 = i * 128
            mk.dma("sync", xt[:], xb_d[r0:r0 + 128, :], chan=1)
            if variant == "post_s3":
                u_ln(mk, xt[:], lnout, stats[:, 0:4], 1024, eps_ap=eps128[:], sq_ap=lnout)
                mk.v(lambda E: E.tensor_add(xt[:], xt[:], lnout), reads=[xt[:], lnout], writes=[xt[:]])
                mk.v(lambda E: E.tensor_add(xt[:], xt[:], brow_bc[:]), reads=[xt[:], brow_bc[:]], writes=[xt[:]])
            if has_p:
                mk.dma("sync", pb[:], p0_d[r0:r0 + 128, :], chan=1)
                mk.v(lambda E: E.tensor_add(xt[:], xt[:], pb[:]), reads=[xt[:], pb[:]], writes=[xt[:]])
                mk.dma("sync", pb[:], p1_d[r0:r0 + 128, :], chan=1)
                mk.v(lambda E: E.tensor_add(xt[:], xt[:], pb[:]), reads=[xt[:], pb[:]], writes=[xt[:]])
            u_ln(mk, xt[:], normed, stats[:, 0:4], 1024, eps_ap=eps128[:], sq_ap=hf[:, 0:1024])
            u_transpose(mk, lambda j: normed[:, j * 128:(j + 1) * 128],
                        lambda j: xf[:, j * 128:(j + 1) * 128], 8, identb[:], psT)
            for c in range(8):
                pcur = ph if c % 2 == 0 else ph2
                u_mm(mk, pcur[:],
                     [xf[:, k * 128:(k + 1) * 128] for k in range(8)],
                     [wfc[:, k * 4096 + c * 512:k * 4096 + (c + 1) * 512] for k in range(8)])
                mk.s(lambda E, c=c, pcur=pcur: E.activation(out=hbuf[:, c * 512:(c + 1) * 512],
                                                            in_=pcur[:], func=AF.Gelu),
                     reads=[pcur[:]], writes=[hbuf[:, c * 512:(c + 1) * 512]])
            mk.v(lambda E: E.reduce_sum(stats[:, 0:1], hbuf[:], AX.X), reads=[hbuf[:]], writes=[stats[:, 0:1]])
            mk.v(lambda E: E.tensor_single_scalar(stats[:, 0:1], stats[:, 0:1], -1.0 / 4096, ALU.mult),
                 reads=[stats[:, 0:1]], writes=[stats[:, 0:1]])
            mk.s(lambda E: E.activation(out=hf[:, 0:2048], in_=hbuf[:, 0:2048], func=AF.Square,
                                        bias=stats[:, 0:1], accum_out=stats[:, 4:5]),
                 reads=[hbuf[:, 0:2048], stats[:, 0:1]], writes=[hf[:, 0:2048], stats[:, 4:5]])
            mk.s(lambda E: E.activation(out=hf[:, 2048:4096], in_=hbuf[:, 2048:4096], func=AF.Square,
                                        bias=stats[:, 0:1], accum_out=stats[:, 5:6]),
                 reads=[hbuf[:, 2048:4096], stats[:, 0:1]], writes=[hf[:, 2048:4096], stats[:, 5:6]])
            mk.v(lambda E: E.tensor_add(stats[:, 1:2], stats[:, 4:5], stats[:, 5:6]),
                 reads=[stats[:, 4:6]], writes=[stats[:, 1:2]])
            mk.s(lambda E: E.activation(out=stats[:, 2:3], in_=stats[:, 1:2], func=AF.Ln,
                                        scale=1.0 / 4096, bias=eps128[:]),
                 reads=[stats[:, 1:2], eps128[:]], writes=[stats[:, 2:3]])
            mk.s(lambda E: E.activation(out=stats[:, 2:3], in_=stats[:, 2:3], func=AF.Exp, scale=-0.5),
                 reads=[stats[:, 2:3]], writes=[stats[:, 2:3]])
            mk.v(lambda E: E.tensor_mul(stats[:, 3:4], stats[:, 0:1], stats[:, 2:3]),
                 reads=[stats[:, 0:1], stats[:, 2:3]], writes=[stats[:, 3:4]])
            mk.v(lambda E: E.tensor_scalar(hbuf[:], hbuf[:], stats[:, 2:3], stats[:, 3:4],
                                           ALU.mult, ALU.add),
                 reads=[hbuf[:], stats[:, 2:4]], writes=[hbuf[:]])
            mk.v(lambda E: E.tensor_mul(hbuf[:], hbuf[:], lnw_bc[:]), reads=[hbuf[:], lnw_bc[:]], writes=[hbuf[:]])
            mk.v(lambda E: E.tensor_add(hbuf[:], hbuf[:], lnb_bc[:]), reads=[hbuf[:], lnb_bc[:]], writes=[hbuf[:]])
            u_transpose(mk, lambda j: hbuf[:, j * 128:(j + 1) * 128],
                        lambda j: hf[:, j * 128:(j + 1) * 128], 32, identb[:], psT)
            for c in range(2):
                u_mm(mk, po[:],
                     [hf[:, k * 128:(k + 1) * 128] for k in range(32)],
                     [wpr[:, k * 1024 + c * 512:k * 1024 + (c + 1) * 512] for k in range(32)])
                mk.v(lambda E, c=c: E.tensor_add(xt[:, c * 512:(c + 1) * 512], po[:],
                                                 xt[:, c * 512:(c + 1) * 512]),
                     reads=[po[:], xt[:, c * 512:(c + 1) * 512]], writes=[xt[:, c * 512:(c + 1) * 512]])
            mk.dma("gpsimd", xo_d[r0:r0 + 128, :], xt[:])
        mk.emit()
    return nc


def ffn_inmaps(x_full, p0_full, p1_full, brow, wfcT, wprojT, lnw, lnb, consts, variant):
    ins = []
    for c in range(8):
        b, h = c // 2, c % 2
        r0 = h * 2048
        m = dict(consts)
        m["xb"] = np.ascontiguousarray(x_full[b, r0:r0 + 2048])
        if variant in ("first", "post_s3"):
            m["p0"] = np.ascontiguousarray(p0_full[c])
            m["p1"] = np.ascontiguousarray(p1_full[c])
        if variant == "post_s3":
            m["brow"] = brow.reshape(1, 1024)
        m["wfcT"] = wfcT
        m["wprojT"] = wprojT
        m["lnw"] = lnw.reshape(1, 4096)
        m["lnb"] = lnb.reshape(1, 4096)
        ins.append(m)
    return ins



def build_attn_gqa():
    nc = bass.Bass("TRN2", target_bir_lowering=False)
    x_d = nc.dram_tensor("x", [4096, 1024], F32, kind="ExternalInput").ap()
    wq_d = nc.dram_tensor("wq", [1024, 512], F32, kind="ExternalInput").ap()
    wk_d = nc.dram_tensor("wk", [1024, 128], F32, kind="ExternalInput").ap()
    wv_d = nc.dram_tensor("wv", [1024, 128], F32, kind="ExternalInput").ap()
    wo_d = nc.dram_tensor("wo", [512, 1024], F32, kind="ExternalInput").ap()
    cosq_d = nc.dram_tensor("cosq", [4096, 256], F32, kind="ExternalInput").ap()
    sinq_d = nc.dram_tensor("sinq", [4096, 256], F32, kind="ExternalInput").ap()
    cosk_d = nc.dram_tensor("cosk", [4096, 64], F32, kind="ExternalInput").ap()
    sink_d = nc.dram_tensor("sink", [4096, 64], F32, kind="ExternalInput").ap()
    tri_d = nc.dram_tensor("tri", [128, 128], F32, kind="ExternalInput").ap()
    identb_d, onesf_d = std_consts(nc)
    po_d = nc.dram_tensor("po", [4096, 1024], F32, kind="ExternalOutput").ap()

    mk = MK(nc)
    es = contextlib.ExitStack()
    with es:
        identb, onesf = load_consts(mk, es, nc, identb_d, onesf_d)
        eps128 = alloc(es, nc, "eps128", [128, 1])
        mk.v(lambda E: E.memset(eps128[:], 1e-5), writes=[eps128[:]])
        wstage = alloc(es, nc, "wstage", [128, 1024])
        wq_sb = alloc(es, nc, "wq_sb", [128, 8 * 512], BF16)
        wk_sb = alloc(es, nc, "wk_sb", [128, 8 * 128], BF16)
        wv_sb = alloc(es, nc, "wv_sb", [128, 8 * 128], BF16)
        wo4 = [alloc(es, nc, f"wo4_{j}", [128, 1024], BF16) for j in range(4)]
        for (wd, wsb, ncols, nk) in ((wq_d, wq_sb, 512, 8), (wk_d, wk_sb, 128, 8),
                                     (wv_d, wv_sb, 128, 8)):
            for k in range(nk):
                st = wstage[:, 0:ncols]
                mk.dma("sync", st, wd[k * 128:(k + 1) * 128, :])
                dst = wsb[:, k * ncols:(k + 1) * ncols]
                mk.v(lambda E, st=st, dst=dst: E.tensor_copy(dst, st), reads=[st], writes=[dst])
        for j in range(4):
            st = wstage[:, 0:1024]
            mk.dma("sync", st, wo_d[j * 128:(j + 1) * 128, :])
            dst = wo4[j][:]
            mk.v(lambda E, st=st, dst=dst: E.tensor_copy(dst, st), reads=[st], writes=[dst])
        tri = alloc(es, nc, "tri_sb", [128, 128])
        mk.dma("sync", tri[:], tri_d)

        xtile = alloc(es, nc, "xtile", [128, 1024])
        normed = alloc(es, nc, "normed", [128, 1024], BF16)
        stats = alloc(es, nc, "stats", [128, 8])
        xf = alloc(es, nc, "xf", [128, 1024], BF16)
        qf32 = alloc(es, nc, "qf32", [128, 512])
        kf32 = alloc(es, nc, "kf32", [128, 128])
        ta = alloc(es, nc, "ta", [128, 256])
        tb = alloc(es, nc, "tb", [128, 256])
        qro = alloc(es, nc, "qro", [128, 512], BF16)
        kro = alloc(es, nc, "kro", [128, 128], BF16)
        cq = alloc(es, nc, "cq", [128, 256])
        sq = alloc(es, nc, "sq", [128, 256])
        ck = alloc(es, nc, "ck", [128, 64])
        sk = alloc(es, nc, "sk", [128, 64])
        Qf = [alloc(es, nc, f"Qf{j}", [128, 4096], BF16) for j in range(4)]
        Kf = alloc(es, nc, "Kf", [128, 4096], BF16)
        vt = alloc(es, nc, "vt", [128, 4096], BF16)
        S = alloc(es, nc, "S", [128, 4096])
        P = alloc(es, nc, "P", [128, 4096], BF16)
        PT = alloc(es, nc, "PT", [128, 16384], BF16)
        ctf = [alloc(es, nc, f"ctf{j}", [128, 4096], BF16) for j in range(4)]
        outsb = alloc(es, nc, "outsb", [128, 1024])
        ph = alloc(es, nc, "ph", [128, 512], psum=True)
        psT = alloc(es, nc, "psT", [128, 1024], BF16, psum=True)
        psc = alloc(es, nc, "psc", [128, 2048], psum=True)
        pctx = alloc(es, nc, "pctx", [64, 512], psum=True)
        ph2c = alloc(es, nc, "ph2c", [128, 512], psum=True)

        for i in range(32):
            r0 = i * 128
            mk.dma("sync", xtile[:], x_d[r0:r0 + 128, :], chan=1)
            u_ln(mk, xtile[:], normed[:], stats[:, 0:4], 1024, eps_ap=eps128[:], sq_ap=S[:, 0:1024])
            u_transpose(mk, lambda j: normed[:, j * 128:(j + 1) * 128],
                        lambda j: xf[:, j * 128:(j + 1) * 128], 8, identb[:], psT)
            u_mm(mk, ph[:, 0:512],
                 [xf[:, k * 128:(k + 1) * 128] for k in range(8)],
                 [wq_sb[:, k * 512:(k + 1) * 512] for k in range(8)])
            mk.v(lambda E: E.tensor_copy(qf32[:], ph[:, 0:512]), reads=[ph[:, 0:512]], writes=[qf32[:]])
            u_mm(mk, ph[:, 0:128],
                 [xf[:, k * 128:(k + 1) * 128] for k in range(8)],
                 [wk_sb[:, k * 128:(k + 1) * 128] for k in range(8)])
            mk.v(lambda E: E.tensor_copy(kf32[:], ph[:, 0:128]), reads=[ph[:, 0:128]], writes=[kf32[:]])
            u_mm(mk, ph[:, 0:128],
                 [xf[:, k * 128:(k + 1) * 128] for k in range(8)],
                 [wv_sb[:, k * 128:(k + 1) * 128] for k in range(8)])
            vdst = vt[:, i * 128:(i + 1) * 128]
            mk.s(lambda E, vdst=vdst: E.activation(out=vdst, in_=ph[:, 0:128], func=AF.Copy),
                 reads=[ph[:, 0:128]], writes=[vdst])
            mk.dma("sync", cq[:], cosq_d[r0:r0 + 128, :], chan=1)
            mk.dma("sync", sq[:], sinq_d[r0:r0 + 128, :], chan=1)
            mk.dma("sync", ck[:], cosk_d[r0:r0 + 128, :], chan=1)
            mk.dma("sync", sk[:], sink_d[r0:r0 + 128, :], chan=1)
            q3 = qf32[:].rearrange("p (h c) -> p h c", c=64)
            c3 = cq[:].rearrange("p (h c) -> p h c", c=32)
            s3 = sq[:].rearrange("p (h c) -> p h c", c=32)
            ta3 = ta[:].rearrange("p (h c) -> p h c", c=32)
            tb3 = tb[:].rearrange("p (h c) -> p h c", c=32)
            o3 = qro[:].rearrange("p (h c) -> p h c", c=64)
            mk.v(lambda E: E.tensor_tensor(ta3, q3[:, :, 0:32], c3, ALU.mult),
                 reads=[qf32[:], cq[:]], writes=[ta[:]])
            mk.v(lambda E: E.tensor_tensor(tb3, q3[:, :, 32:64], s3, ALU.mult),
                 reads=[qf32[:], sq[:]], writes=[tb[:]])
            mk.v(lambda E: E.tensor_tensor(o3[:, :, 0:32], ta3, tb3, ALU.subtract),
                 reads=[ta[:], tb[:]], writes=[qro[:]])
            mk.v(lambda E: E.tensor_tensor(ta3, q3[:, :, 0:32], s3, ALU.mult),
                 reads=[qf32[:], sq[:]], writes=[ta[:]])
            mk.v(lambda E: E.tensor_tensor(tb3, q3[:, :, 32:64], c3, ALU.mult),
                 reads=[qf32[:], cq[:]], writes=[tb[:]])
            mk.v(lambda E: E.tensor_tensor(o3[:, :, 32:64], ta3, tb3, ALU.add),
                 reads=[ta[:], tb[:]], writes=[qro[:]])
            k3 = kf32[:].rearrange("p (h c) -> p h c", c=64)
            ck3 = ck[:].rearrange("p (h c) -> p h c", c=32)
            sk3 = sk[:].rearrange("p (h c) -> p h c", c=32)
            ka3 = ta[:, 0:64].rearrange("p (h c) -> p h c", c=32)
            kb3 = tb[:, 0:64].rearrange("p (h c) -> p h c", c=32)
            ko3 = kro[:].rearrange("p (h c) -> p h c", c=64)
            mk.v(lambda E: E.tensor_tensor(ka3, k3[:, :, 0:32], ck3, ALU.mult),
                 reads=[kf32[:], ck[:]], writes=[ta[:, 0:64]])
            mk.v(lambda E: E.tensor_tensor(kb3, k3[:, :, 32:64], sk3, ALU.mult),
                 reads=[kf32[:], sk[:]], writes=[tb[:, 0:64]])
            mk.v(lambda E: E.tensor_tensor(ko3[:, :, 0:32], ka3, kb3, ALU.subtract),
                 reads=[ta[:, 0:64], tb[:, 0:64]], writes=[kro[:]])
            mk.v(lambda E: E.tensor_tensor(ka3, k3[:, :, 0:32], sk3, ALU.mult),
                 reads=[kf32[:], sk[:]], writes=[ta[:, 0:64]])
            mk.v(lambda E: E.tensor_tensor(kb3, k3[:, :, 32:64], ck3, ALU.mult),
                 reads=[kf32[:], ck[:]], writes=[tb[:, 0:64]])
            mk.v(lambda E: E.tensor_tensor(ko3[:, :, 32:64], ka3, kb3, ALU.add),
                 reads=[ta[:, 0:64], tb[:, 0:64]], writes=[kro[:]])
            for j in range(4):
                src_b = qro[:, j * 128:(j + 1) * 128]
                mk.t(lambda E, src_b=src_b: E.transpose(psT[:, 0:128], src_b, identb[:]),
                     reads=[src_b, identb[:]], writes=[psT[:, 0:128]])
                for half in range(2):
                    h = 2 * j + half
                    tns, rb = h % 4, 64 * (h // 4)
                    dst = Qf[tns][rb:rb + 64, i * 128:(i + 1) * 128]
                    srcp = psT[64 * half:64 * half + 64, 0:128]
                    mk.v(lambda E, dst=dst, srcp=srcp: E.tensor_copy(dst, srcp),
                         reads=[srcp], writes=[dst])
            mk.t(lambda E: E.transpose(psT[:, 0:128], kro[:, 0:128], identb[:]),
                 reads=[kro[:, 0:128], identb[:]], writes=[psT[:, 0:128]])
            dst = Kf[:, i * 128:(i + 1) * 128]
            mk.v(lambda E, dst=dst: E.tensor_copy(dst, psT[:, 0:128]),
                 reads=[psT[:, 0:128]], writes=[dst])

        for h in range(8):
            vcol = (h // 4) * 64
            for qc in range(8):
                KN = 4 * qc + 4
                for qs in range(4):
                    qt = 4 * qc + qs
                    KEXT = qt + 1
                    q0 = qt * 128
                    qb = 64 * (h // 4)
                    lhs = Qf[h % 4][qb:qb + 64, q0:q0 + 128]
                    for p0 in range(0, KEXT * 128, 2048):
                        p1 = min(KEXT * 128, p0 + 2048)
                        for c0 in range(p0, p1, 512):
                            c1 = min(p1, c0 + 512)
                            rhs = Kf[qb:qb + 64, c0:c1]
                            dst = psc[:, c0 - p0:c1 - p0]
                            mk.t(lambda E, dst=dst, lhs=lhs, rhs=rhs: E.matmul(dst, lhs, rhs,
                                                                              start=True, stop=True),
                                 reads=[lhs, rhs], writes=[dst])
                        sap = S[:, p0:p1]
                        pap = psc[:, 0:p1 - p0]
                        mk.s(lambda E, sap=sap, pap=pap: E.activation(out=sap, in_=pap, func=AF.Copy,
                                                                     scale=8.0),
                             reads=[pap], writes=[sap])
                    dap = S[:, (KEXT - 1) * 128:KEXT * 128]
                    mk.v(lambda E, dap=dap: E.tensor_add(dap, dap, tri[:]),
                         reads=[dap, tri[:]], writes=[dap])
                    sall = S[:, 0:KEXT * 128]
                    mk.v(lambda E, sall=sall: E.reduce_max(stats[:, 0:1], sall, AX.X),
                         reads=[sall], writes=[stats[:, 0:1]])
                    mk.v(lambda E: E.tensor_single_scalar(stats[:, 0:1], stats[:, 0:1], -1.0, ALU.mult),
                         reads=[stats[:, 0:1]], writes=[stats[:, 0:1]])
                    pall = P[:, 0:KEXT * 128]
                    mk.s(lambda E, sall=sall, pall=pall: E.activation(out=pall, in_=sall, func=AF.Exp,
                                                                     bias=stats[:, 0:1],
                                                                     accum_out=stats[:, 1:2]),
                         reads=[sall, stats[:, 0:1]], writes=[pall, stats[:, 1:2]])
                    mk.v(lambda E: E.reciprocal(stats[:, 2:3], stats[:, 1:2]),
                         reads=[stats[:, 1:2]], writes=[stats[:, 2:3]])
                    mk.v(lambda E, pall=pall: E.tensor_scalar_mul(pall, pall, stats[:, 2:3]),
                         reads=[pall, stats[:, 2:3]], writes=[pall])
                    for kt0 in range(0, KEXT, 8):
                        kt1 = min(KEXT, kt0 + 8)
                        for kt in range(kt0, kt1):
                            src = P[:, kt * 128:(kt + 1) * 128]
                            slot = kt - kt0
                            mk.t(lambda E, src=src, slot=slot: E.transpose(
                                    psT[:, slot * 128:(slot + 1) * 128], src, identb[:]),
                                 reads=[src, identb[:]], writes=[psT[:, slot * 128:(slot + 1) * 128]])
                        nsl = kt1 - kt0
                        src3 = psT[:, 0:nsl * 128].rearrange("p (k c) -> p k c", c=128)
                        dst3 = PT[:].rearrange("p (k q c) -> p k q c", q=4, c=128)[:, kt0:kt1, qs, :]
                        mk.v(lambda E, src3=src3, dst3=dst3, nsl=nsl: E.tensor_copy(dst3, src3),
                             reads=[psT[:, 0:nsl * 128]],
                             writes=[PT[:, kt0 * 512:(kt1 - 1) * 512 + (qs + 1) * 128]])
                for kt in range(4 * qc + 1, KN):
                    nz = (kt - 4 * qc) * 128
                    zap = PT[:, kt * 512:kt * 512 + nz]
                    mk.v(lambda E, zap=zap: E.memset(zap, 0.0), writes=[zap])
                for kt in range(KN):
                    lhsv = vt[:, kt * 128 + vcol:kt * 128 + vcol + 64]
                    rhsp = PT[:, kt * 512:(kt + 1) * 512]
                    mk.t(lambda E, lhsv=lhsv, rhsp=rhsp, kt=kt, KN=KN: E.matmul(
                            pctx[:, 0:512], lhsv, rhsp, start=(kt == 0), stop=(kt == KN - 1)),
                         reads=[lhsv, rhsp], writes=[pctx[:, 0:512]])
                cdst = ctf[h // 2][64 * (h % 2):64 * (h % 2) + 64, qc * 512:(qc + 1) * 512]
                mk.v(lambda E, cdst=cdst: E.tensor_copy(cdst, pctx[:, 0:512]),
                     reads=[pctx[:, 0:512]], writes=[cdst])

        for i in range(32):
            for ch in range(2):
                u_mm(mk, ph[:, 0:512],
                     [ctf[j][0:64, i * 128:(i + 1) * 128] for j in range(4)],
                     [wo4[j][0:64, ch * 512:(ch + 1) * 512] for j in range(4)])
                oap = outsb[:, ch * 512:(ch + 1) * 512]
                mk.v(lambda E, oap=oap: E.tensor_copy(oap, ph[:, 0:512]),
                     reads=[ph[:, 0:512]], writes=[oap])
                u_mm(mk, ph2c[:, 0:512],
                     [ctf[j][64:128, i * 128:(i + 1) * 128] for j in range(4)],
                     [wo4[j][64:128, ch * 512:(ch + 1) * 512] for j in range(4)])
                mk.v(lambda E, oap=oap: E.tensor_add(oap, oap, ph2c[:, 0:512]),
                     reads=[oap, ph2c[:, 0:512]], writes=[oap])
            mk.dma("gpsimd", po_d[i * 128:(i + 1) * 128, :], outsb[:])
        mk.emit()
    return nc


def rope_tables_s1():
    inv = (1.0 / (10000.0 ** (np.arange(0, 64, 2, dtype=np.float32) / 64))).astype(np.float32)
    pos = np.arange(4096, dtype=np.float32)
    fr = (pos[:, None] * inv[None, :]).astype(np.float32)
    return np.cos(fr).astype(np.float32), np.sin(fr).astype(np.float32)


def attn_gqa_inmaps(x_full, wq, wk, wv, wo, consts):
    cos, sin = rope_tables_s1()
    cosq = np.tile(cos, (1, 8)).copy()
    sinq = np.tile(sin, (1, 8)).copy()
    cosk = np.tile(cos, (1, 2)).copy()
    sink = np.tile(sin, (1, 2)).copy()
    tri = np.triu(np.full((128, 128), -1e30, np.float32), k=1)
    ins = []
    for c in range(8):
        b, hh = c // 2, c % 2
        m = dict(consts)
        m["x"] = np.ascontiguousarray(x_full[b])
        m["wq"] = np.ascontiguousarray(wq[:, hh * 512:(hh + 1) * 512])
        m["wk"] = np.ascontiguousarray(wk[:, hh * 128:(hh + 1) * 128])
        m["wv"] = np.ascontiguousarray(wv[:, hh * 128:(hh + 1) * 128])
        m["wo"] = np.ascontiguousarray(wo[hh * 512:(hh + 1) * 512, :])
        m["cosq"] = cosq; m["sinq"] = sinq; m["cosk"] = cosk; m["sink"] = sink
        m["tri"] = tri
        ins.append(m)
    return ins



def build_gau():
    nc = bass.Bass("TRN2", target_bir_lowering=False)
    x_d = nc.dram_tensor("x", [4096, 1024], F32, kind="ExternalInput").ap()
    lng_d = nc.dram_tensor("lng", [1, 1024], F32, kind="ExternalInput").ap()
    lnbg_d = nc.dram_tensor("lnbg", [1, 1024], F32, kind="ExternalInput").ap()
    whv_d = nc.dram_tensor("whv", [1024, 1024], F32, kind="ExternalInput").ap()
    whg_d = nc.dram_tensor("whg", [1024, 1024], F32, kind="ExternalInput").ap()
    bhv_d = nc.dram_tensor("bhv", [1, 1024], F32, kind="ExternalInput").ap()
    bhg_d = nc.dram_tensor("bhg", [1, 1024], F32, kind="ExternalInput").ap()
    wqk_d = nc.dram_tensor("wqk", [1024, 128], F32, kind="ExternalInput").ap()
    bqk_d = nc.dram_tensor("bqk", [1, 128], F32, kind="ExternalInput").ap()
    osg_d = nc.dram_tensor("osg", [4, 128], F32, kind="ExternalInput").ap()
    osb_d = nc.dram_tensor("osb", [4, 128], F32, kind="ExternalInput").ap()
    cos_d = nc.dram_tensor("coss3", [4096, 64], F32, kind="ExternalInput").ap()
    sin_d = nc.dram_tensor("sins3", [4096, 64], F32, kind="ExternalInput").ap()
    wout_d = nc.dram_tensor("wout", [1024, 1024], F32, kind="ExternalInput").ap()
    identb_d, onesf_d = std_consts(nc)
    po_d = nc.dram_tensor("po", [4096, 1024], F32, kind="ExternalOutput").ap()
    gsc_d = nc.dram_tensor("gscratch", [4096, 1024], BF16, kind="Internal").ap()

    mk = MK(nc)
    es = contextlib.ExitStack()
    with es:
        identb, onesf = load_consts(mk, es, nc, identb_d, onesf_d)
        eps128 = alloc(es, nc, "eps128", [128, 1])
        mk.v(lambda E: E.memset(eps128[:], 1e-5), writes=[eps128[:]])
        onesb = alloc(es, nc, "onesb", [1, 128], BF16)
        mk.v(lambda E: E.tensor_copy(onesb[:], onesf[:]), reads=[onesf[:]], writes=[onesb[:]])
        wstage = alloc(es, nc, "wstage", [128, 1024])
        whv = alloc(es, nc, "whv_s", [128, 8 * 1024], BF16)
        whg = alloc(es, nc, "whg_s", [128, 8 * 1024], BF16)
        wqk = alloc(es, nc, "wqk_s", [128, 8 * 128], BF16)
        wout = alloc(es, nc, "wout_s", [128, 8 * 1024], BF16)
        for (wd, wsb, ncols) in ((whv_d, whv, 1024), (whg_d, whg, 1024), (wqk_d, wqk, 128),
                                 (wout_d, wout, 1024)):
            for k in range(8):
                st = wstage[:, 0:ncols]
                mk.dma("sync", st, wd[k * 128:(k + 1) * 128, :])
                dst = wsb[:, k * ncols:(k + 1) * ncols]
                mk.v(lambda E, st=st, dst=dst: E.tensor_copy(dst, st), reads=[st], writes=[dst])
        rowst = alloc(es, nc, "rowst", [1, 1024])
        bhv = alloc(es, nc, "bhv_b", [1, 1024], BF16)
        bhg = alloc(es, nc, "bhg_b", [1, 1024], BF16)
        bqk = alloc(es, nc, "bqk_b", [1, 128], BF16)
        for (rd, rb, ncols) in ((bhv_d, bhv, 1024), (bhg_d, bhg, 1024), (bqk_d, bqk, 128)):
            st = rowst[:, 0:ncols]
            mk.dma("sync", st, rd)
            mk.v(lambda E, st=st, rb=rb: E.tensor_copy(rb[:], st), reads=[st], writes=[rb[:]])
        psb = alloc(es, nc, "psb", [128, 512], psum=True)
        lng_bc = alloc(es, nc, "lng_bc", [128, 1024])
        lnb_bc = alloc(es, nc, "lnb_bc", [128, 1024])
        mk.dma("sync", rowst[:], lng_d)
        u_bcast_row(mk, rowst[:], lng_bc[:], onesf[:], psb[:], 1024)
        mk.dma("sync", rowst[:], lnbg_d)
        u_bcast_row(mk, rowst[:], lnb_bc[:], onesf[:], psb[:], 1024)
        osg_bc = [alloc(es, nc, f"osg{i}", [128, 128]) for i in range(4)]
        osb_bc = [alloc(es, nc, f"osb{i}", [128, 128]) for i in range(4)]
        for i in range(4):
            st = rowst[:, 0:128]
            mk.dma("sync", st, osg_d[i:i + 1, :])
            u_bcast_row(mk, st, osg_bc[i][:], onesf[:], psb[:], 128)
            mk.dma("sync", st, osb_d[i:i + 1, :])
            u_bcast_row(mk, st, osb_bc[i][:], onesf[:], psb[:], 128)

        xtile = alloc(es, nc, "xtile", [128, 1024])
        lnt = alloc(es, nc, "lnt", [128, 1024])
        normed = alloc(es, nc, "normed", [128, 1024], BF16)
        stats = alloc(es, nc, "stats", [128, 8])
        xf = alloc(es, nc, "xf", [128, 1024], BF16)
        vtok = alloc(es, nc, "vtok", [128, 32 * 1024], BF16)
        gt = alloc(es, nc, "gt", [128, 1024], BF16)
        qk32 = alloc(es, nc, "qk32", [128, 128])
        vr = alloc(es, nc, "vr", [128, 128])
        ta = alloc(es, nc, "ta", [128, 64])
        tb = alloc(es, nc, "tb", [128, 64])
        rov = alloc(es, nc, "rov", [128, 128], BF16)
        cq = alloc(es, nc, "cq", [128, 64])
        sq = alloc(es, nc, "sq", [128, 64])
        Qqf = alloc(es, nc, "Qqf", [128, 4096], BF16)
        Qlf = alloc(es, nc, "Qlf", [128, 4096], BF16)
        Kqf = alloc(es, nc, "Kqf", [128, 4096], BF16)
        klin = alloc(es, nc, "klin", [128, 4096], BF16)
        linKV = alloc(es, nc, "linKV", [128, 1024], BF16)
        simt = alloc(es, nc, "simt", [128, 256])
        Pq = alloc(es, nc, "Pq", [128, 256], BF16)
        PTq = alloc(es, nc, "PTq", [128, 256], BF16)
        outg = alloc(es, nc, "outg", [128, 1024], BF16)
        ogf = alloc(es, nc, "ogf", [128, 1024], BF16)
        outsb = alloc(es, nc, "outsb", [128, 1024])
        ph = alloc(es, nc, "ph", [128, 512], psum=True)
        psT = alloc(es, nc, "psT", [128, 1024], BF16, psum=True)
        pq = alloc(es, nc, "pq", [128, 512], psum=True)

        for i in range(32):
            r0 = i * 128
            mk.dma("sync", xtile[:], x_d[r0:r0 + 128, :], chan=1)
            u_ln(mk, xtile[:], lnt[:], stats[:, 0:4], 1024, eps_ap=eps128[:], sq_ap=lnt[:])
            mk.v(lambda E: E.tensor_mul(lnt[:], lnt[:], lng_bc[:]), reads=[lnt[:], lng_bc[:]], writes=[lnt[:]])
            mk.v(lambda E: E.tensor_add(normed[:], lnt[:], lnb_bc[:]), reads=[lnt[:], lnb_bc[:]], writes=[normed[:]])
            u_transpose(mk, lambda j: normed[:, j * 128:(j + 1) * 128],
                        lambda j: xf[:, j * 128:(j + 1) * 128], 8, identb[:], psT)
            for (wsb, brow, dst_kind) in ((whv, bhv, "v"), (whg, bhg, "g"), (wqk, bqk, "qk")):
                ncols = 128 if dst_kind == "qk" else 1024
                for c0 in range(0, ncols, 512):
                    c1 = min(ncols, c0 + 512)
                    lhs = [xf[:, k * 128:(k + 1) * 128] for k in range(8)] + [onesb[:]]
                    rhs = [wsb[:, k * ncols + c0:k * ncols + c1] for k in range(8)] + [brow[:, c0:c1]]
                    u_mm(mk, ph[:, 0:c1 - c0], lhs, rhs)
                    if dst_kind == "v":
                        dst = vtok[:, i * 1024 + c0:i * 1024 + c1]
                    elif dst_kind == "g":
                        dst = gt[:, c0:c1]
                    else:
                        dst = qk32[:, c0:c1]
                    fn = AF.Silu
                    mk.s(lambda E, dst=dst, c1=c1, c0=c0, fn=fn: E.activation(out=dst, in_=ph[:, 0:c1 - c0],
                                                                             func=fn),
                         reads=[ph[:, 0:c1 - c0]], writes=[dst])
            mk.dma("gpsimd", gsc_d[r0:r0 + 128, :], gt[:])
            mk.dma("sync", cq[:], cos_d[r0:r0 + 128, :], chan=1)
            mk.dma("sync", sq[:], sin_d[r0:r0 + 128, :], chan=1)
            for vi, store in ((0, "Qqf"), (1, "Qlf"), (2, "Kqf"), (3, "klin")):
                mk.v(lambda E, vi=vi: E.tensor_tensor(vr[:], qk32[:], osg_bc[vi][:], ALU.mult),
                     reads=[qk32[:], osg_bc[vi][:]], writes=[vr[:]])
                mk.v(lambda E, vi=vi: E.tensor_add(vr[:], vr[:], osb_bc[vi][:]),
                     reads=[vr[:], osb_bc[vi][:]], writes=[vr[:]])
                mk.v(lambda E: E.tensor_tensor(ta[:], vr[:, 0:64], cq[:], ALU.mult),
                     reads=[vr[:], cq[:]], writes=[ta[:]])
                mk.v(lambda E: E.tensor_tensor(tb[:], vr[:, 64:128], sq[:], ALU.mult),
                     reads=[vr[:], sq[:]], writes=[tb[:]])
                mk.v(lambda E: E.tensor_tensor(rov[:, 0:64], ta[:], tb[:], ALU.subtract),
                     reads=[ta[:], tb[:]], writes=[rov[:, 0:64]])
                mk.v(lambda E: E.tensor_tensor(ta[:], vr[:, 64:128], cq[:], ALU.mult),
                     reads=[vr[:], cq[:]], writes=[ta[:]])
                mk.v(lambda E: E.tensor_tensor(tb[:], vr[:, 0:64], sq[:], ALU.mult),
                     reads=[vr[:], sq[:]], writes=[tb[:]])
                mk.v(lambda E: E.tensor_tensor(rov[:, 64:128], ta[:], tb[:], ALU.add),
                     reads=[ta[:], tb[:]], writes=[rov[:, 64:128]])
                if store == "klin":
                    dst = klin[:, i * 128:(i + 1) * 128]
                    mk.v(lambda E, dst=dst: E.tensor_copy(dst, rov[:]), reads=[rov[:]], writes=[dst])
                else:
                    tgt = {"Qqf": Qqf, "Qlf": Qlf, "Kqf": Kqf}[store]
                    mk.t(lambda E: E.transpose(psT[:, 0:128], rov[:], identb[:]),
                         reads=[rov[:], identb[:]], writes=[psT[:, 0:128]])
                    dst = tgt[:, i * 128:(i + 1) * 128]
                    mk.v(lambda E, dst=dst: E.tensor_copy(dst, psT[:, 0:128]),
                         reads=[psT[:, 0:128]], writes=[dst])

        for c0 in range(0, 1024, 512):
            u_mm(mk, ph[:, 0:512],
                 [klin[:, i * 128:(i + 1) * 128] for i in range(32)],
                 [vtok[:, i * 1024 + c0:i * 1024 + c0 + 512] for i in range(32)])
            dst = linKV[:, c0:c0 + 512]
            mk.s(lambda E, dst=dst: E.activation(out=dst, in_=ph[:, 0:512], func=AF.Copy,
                                                 scale=1.0 / 4096),
                 reads=[ph[:, 0:512]], writes=[dst])

        for g in range(16):
            for qs in range(2):
                q0 = g * 256 + qs * 128
                lhs = Qqf[:, q0:q0 + 128]
                rhs = Kqf[:, g * 256:(g + 1) * 256]
                mk.t(lambda E, lhs=lhs, rhs=rhs: E.matmul(pq[:, 0:256], lhs, rhs, start=True, stop=True),
                     reads=[lhs, rhs], writes=[pq[:, 0:256]])
                mk.s(lambda E: E.activation(out=simt[:], in_=pq[:, 0:256], func=AF.Relu,
                                            scale=1.0 / 256),
                     reads=[pq[:, 0:256]], writes=[simt[:]])
                mk.s(lambda E: E.activation(out=Pq[:], in_=simt[:], func=AF.Square),
                     reads=[simt[:]], writes=[Pq[:]])
                for j in range(2):
                    src = Pq[:, j * 128:(j + 1) * 128]
                    mk.t(lambda E, src=src, j=j: E.transpose(psT[:, j * 128:(j + 1) * 128], src,
                                                             identb[:]),
                         reads=[src, identb[:]], writes=[psT[:, j * 128:(j + 1) * 128]])
                mk.v(lambda E: E.tensor_copy(PTq[:], psT[:, 0:256]), reads=[psT[:, 0:256]], writes=[PTq[:]])
                mk.dma("sync", gt[:], gsc_d[q0:q0 + 128, :], chan=1)
                for c0 in range(0, 1024, 512):
                    lhs_l = [PTq[:, j * 128:(j + 1) * 128] for j in range(2)] + [Qlf[:, q0:q0 + 128]]
                    rhs_l = [vtok[:, (2 * g + j) * 1024 + c0:(2 * g + j) * 1024 + c0 + 512]
                             for j in range(2)] + [linKV[:, c0:c0 + 512]]
                    u_mm(mk, pq[:, 0:512], lhs_l, rhs_l)
                    dst = outg[:, c0:c0 + 512]
                    gap = gt[:, c0:c0 + 512]
                    mk.v(lambda E, dst=dst, gap=gap: E.tensor_tensor(dst, pq[:, 0:512], gap, ALU.mult),
                         reads=[pq[:, 0:512], gap], writes=[dst])
                u_transpose(mk, lambda j: outg[:, j * 128:(j + 1) * 128],
                            lambda j: ogf[:, j * 128:(j + 1) * 128], 8, identb[:], psT)
                for c0 in range(0, 1024, 512):
                    u_mm(mk, ph[:, 0:512],
                         [ogf[:, k * 128:(k + 1) * 128] for k in range(8)],
                         [wout[:, k * 1024 + c0:k * 1024 + c0 + 512] for k in range(8)])
                    oap = outsb[:, c0:c0 + 512]
                    mk.v(lambda E, oap=oap: E.tensor_copy(oap, ph[:, 0:512]),
                         reads=[ph[:, 0:512]], writes=[oap])
                mk.dma("gpsimd", po_d[q0:q0 + 128, :], outsb[:])
        mk.emit()
    return nc


def rope_tables_s3():
    inv = np.power(np.float32(10000.0), (np.arange(64, dtype=np.float32) / 64)).astype(np.float32)
    pos = np.arange(4096, dtype=np.float32)
    ang = (pos[:, None] * inv[None, :]).astype(np.float32)
    return np.cos(ang).astype(np.float32), np.sin(ang).astype(np.float32)


def gau_inmaps(x_full, lng, lnb, wh, bh, wqk, bqk, osg, osb, wout, consts):
    cos, sin = rope_tables_s3()
    ins = []
    for c in range(8):
        b, eh = c // 2, c % 2
        m = dict(consts)
        m["x"] = np.ascontiguousarray(x_full[b])
        m["lng"] = lng.reshape(1, 1024); m["lnbg"] = lnb.reshape(1, 1024)
        m["whv"] = np.ascontiguousarray(wh[:, eh * 1024:(eh + 1) * 1024])
        m["whg"] = np.ascontiguousarray(wh[:, 2048 + eh * 1024:2048 + (eh + 1) * 1024])
        m["bhv"] = np.ascontiguousarray(bh[eh * 1024:(eh + 1) * 1024]).reshape(1, 1024)
        m["bhg"] = np.ascontiguousarray(bh[2048 + eh * 1024:2048 + (eh + 1) * 1024]).reshape(1, 1024)
        m["wqk"] = wqk; m["bqk"] = bqk.reshape(1, 128)
        m["osg"] = osg; m["osb"] = osb
        m["coss3"] = cos; m["sins3"] = sin
        m["wout"] = np.ascontiguousarray(wout[eh * 1024:(eh + 1) * 1024, :])
        ins.append(m)
    return ins



def build_mamba_a():
    nc = bass.Bass("TRN2", target_bir_lowering=False)
    xe_d = nc.dram_tensor("xe", [2176, 1024], F32, kind="ExternalInput").ap()
    iwx_d = [nc.dram_tensor(f"iwx{d}", [1024, 2048], F32, kind="ExternalInput").ap() for d in range(2)]
    iwz_d = [nc.dram_tensor(f"iwz{d}", [1024, 2048], F32, kind="ExternalInput").ap() for d in range(2)]
    cw_d = nc.dram_tensor("cw", [128, 8], F32, kind="ExternalInput").ap()
    cb_d = nc.dram_tensor("cb", [128, 32], F32, kind="ExternalInput").ap()
    cwf_d = nc.dram_tensor("cwf", [128, 128], F32, kind="ExternalInput").ap()
    xpd_d = [nc.dram_tensor(f"xpd{d}", [2048, 64], F32, kind="ExternalInput").ap() for d in range(2)]
    xpb_d = [nc.dram_tensor(f"xpb{d}", [2048, 16], F32, kind="ExternalInput").ap() for d in range(2)]
    xpc_d = [nc.dram_tensor(f"xpc{d}", [2048, 16], F32, kind="ExternalInput").ap() for d in range(2)]
    dtw_d = [nc.dram_tensor(f"dtw{d}", [64, 2048], F32, kind="ExternalInput").ap() for d in range(2)]
    dtb_d = nc.dram_tensor("dtb", [128, 32], F32, kind="ExternalInput").ap()
    identb_d, onesf_d = std_consts(nc)
    dtT_o = [nc.dram_tensor(f"dtT{d}", [2048, 2048], F32, kind="ExternalOutput").ap() for d in range(2)]
    dtxT_o = [nc.dram_tensor(f"dtxT{d}", [2048, 2048], BF16, kind="ExternalOutput").ap() for d in range(2)]
    xcT_o = [nc.dram_tensor(f"xcT{d}", [2048, 2048], BF16, kind="ExternalOutput").ap() for d in range(2)]
    szT_o = [nc.dram_tensor(f"szT{d}", [2048, 2048], BF16, kind="ExternalOutput").ap() for d in range(2)]
    bt_o = [nc.dram_tensor(f"bt{d}", [16, 2048], F32, kind="ExternalOutput").ap() for d in range(2)]
    ct_o = [nc.dram_tensor(f"ct{d}", [16, 2048], F32, kind="ExternalOutput").ap() for d in range(2)]

    mk = MK(nc)
    es = contextlib.ExitStack()
    with es:
        identb, onesf = load_consts(mk, es, nc, identb_d, onesf_d)
        eps128 = alloc(es, nc, "eps128", [128, 1])
        mk.v(lambda E: E.memset(eps128[:], 1e-5), writes=[eps128[:]])
        one128 = alloc(es, nc, "one128", [128, 1])
        mk.v(lambda E: E.memset(one128[:], 1.0), writes=[one128[:]])
        XfE = alloc(es, nc, "XfE", [128, 8 * 2176], BF16)
        xtile = alloc(es, nc, "xtile", [128, 1024])
        lnt = alloc(es, nc, "lnt", [128, 1024])
        normed = alloc(es, nc, "normed", [128, 1024], BF16)
        stats = alloc(es, nc, "stats", [128, 8])
        psT = alloc(es, nc, "psT", [128, 1024], BF16, psum=True)
        for i in range(17):
            r0 = i * 128
            mk.dma("sync", xtile[:], xe_d[r0:r0 + 128, :], chan=1)
            u_ln(mk, xtile[:], normed[:], stats[:, 0:4], 1024, eps_ap=eps128[:], sq_ap=lnt[:])
            u_transpose(mk, lambda j: normed[:, j * 128:(j + 1) * 128],
                        lambda j, i=i: XfE[:, j * 2176 + i * 128:j * 2176 + (i + 1) * 128],
                        8, identb[:], psT)

        wst = alloc(es, nc, "wst", [128, 2048])
        iwx = alloc(es, nc, "iwx_s", [128, 8 * 2048], BF16)
        iwz = alloc(es, nc, "iwz_s", [128, 8 * 2048], BF16)
        xpd = alloc(es, nc, "xpd_s", [128, 16 * 64], BF16)
        xpb = alloc(es, nc, "xpb_s", [128, 16 * 16], BF16)
        xpc = alloc(es, nc, "xpc_s", [128, 16 * 16], BF16)
        dtw = alloc(es, nc, "dtw_s", [64, 2048], BF16)
        cwf = alloc(es, nc, "cwf_s", [128, 128])
        cbs = alloc(es, nc, "cbs", [128, 32])
        dtbs = alloc(es, nc, "dtbs", [128, 32])
        mk.dma("sync", cwf[:], cwf_d)
        mk.dma("sync", cbs[:], cb_d)
        mk.dma("sync", dtbs[:], dtb_d)
        xiT = alloc(es, nc, "xiT", [128, 2176])
        acc1 = alloc(es, nc, "acc1", [128, 2048])
        acc2 = alloc(es, nc, "acc2", [128, 2048])
        xcT = alloc(es, nc, "xcT_t", [128, 2048], BF16)
        dtraw = alloc(es, nc, "dtraw", [64, 2048])
        dtrawb = alloc(es, nc, "dtrawb", [64, 2048], BF16)
        btacc = alloc(es, nc, "btacc", [16, 2048])
        ctacc = alloc(es, nc, "ctacc", [16, 2048])
        dtTt = alloc(es, nc, "dtTt", [128, 2048])
        sp1 = alloc(es, nc, "sp1", [128, 2048])
        dtxt = alloc(es, nc, "dtxt", [128, 2048], BF16)
        ph = alloc(es, nc, "ph", [128, 512], psum=True)
        pd64 = alloc(es, nc, "pd64", [64, 512], psum=True)
        pd16 = alloc(es, nc, "pd16", [16, 512], psum=True)

        for d in range(2):
            for k in range(8):
                st = wst[:, 0:2048]
                mk.dma("sync", st, iwx_d[d][k * 128:(k + 1) * 128, :])
                dst = iwx[:, k * 2048:(k + 1) * 2048]
                mk.v(lambda E, st=st, dst=dst: E.tensor_copy(dst, st), reads=[st], writes=[dst])
                mk.dma("sync", st, iwz_d[d][k * 128:(k + 1) * 128, :])
                dst = iwz[:, k * 2048:(k + 1) * 2048]
                mk.v(lambda E, st=st, dst=dst: E.tensor_copy(dst, st), reads=[st], writes=[dst])
            for k in range(16):
                st = wst[:, 0:64]
                mk.dma("sync", st, xpd_d[d][k * 128:(k + 1) * 128, :])
                dst = xpd[:, k * 64:(k + 1) * 64]
                mk.v(lambda E, st=st, dst=dst: E.tensor_copy(dst, st), reads=[st], writes=[dst])
                st = wst[:, 0:16]
                mk.dma("sync", st, xpb_d[d][k * 128:(k + 1) * 128, :])
                dst = xpb[:, k * 16:(k + 1) * 16]
                mk.v(lambda E, st=st, dst=dst: E.tensor_copy(dst, st), reads=[st], writes=[dst])
                mk.dma("sync", st, xpc_d[d][k * 128:(k + 1) * 128, :])
                dst = xpc[:, k * 16:(k + 1) * 16]
                mk.v(lambda E, st=st, dst=dst: E.tensor_copy(dst, st), reads=[st], writes=[dst])
            st = wst[0:64, 0:2048]
            mk.dma("sync", st, dtw_d[d])
            mk.v(lambda E, st=st: E.tensor_copy(dtw[:], st), reads=[st], writes=[dtw[:]])
            mk.v(lambda E: E.memset(btacc[:], 0.0), writes=[btacc[:]])
            mk.v(lambda E: E.memset(ctacc[:], 0.0), writes=[ctacc[:]])
            mk.v(lambda E: E.memset(dtraw[:], 0.0), writes=[dtraw[:]])

            for dt in range(16):
                for c0 in range(0, 2176, 512):
                    c1 = min(2176, c0 + 512)
                    u_mm(mk, ph[:, 0:c1 - c0],
                         [iwx[:, k * 2048 + dt * 128:k * 2048 + dt * 128 + 128] for k in range(8)],
                         [XfE[:, k * 2176 + c0:k * 2176 + c1] for k in range(8)])
                    dst = xiT[:, c0:c1]
                    mk.v(lambda E, dst=dst, w=c1 - c0: E.tensor_copy(dst, ph[:, 0:w]),
                         reads=[ph[:, 0:c1 - c0]], writes=[dst])
                col = cwf[:, (d * 16 + dt) * 4:(d * 16 + dt) * 4 + 4]
                for j in range(4):
                    off = (61 + j) if d == 0 else (67 - j)
                    sl = xiT[:, off:off + 2048]
                    wj = col[:, j:j + 1]
                    if j == 0:
                        mk.v(lambda E, sl=sl, wj=wj: E.tensor_scalar_mul(acc1[:], sl, wj),
                             reads=[sl, wj], writes=[acc1[:]])
                    else:
                        src_acc = acc1 if j % 2 == 1 else acc2
                        dst_acc = acc2 if j % 2 == 1 else acc1
                        mk.v(lambda E, sl=sl, wj=wj, s=src_acc, dd=dst_acc: E.scalar_tensor_tensor(
                                dd[:], sl, wj, s[:], ALU.mult, ALU.add),
                             reads=[sl, wj, src_acc[:]], writes=[dst_acc[:]])
                cbcol = cbs[:, d * 16 + dt:d * 16 + dt + 1]
                mk.s(lambda E, cbcol=cbcol: E.activation(out=xcT[:], in_=acc2[:], func=AF.Silu,
                                                         bias=cbcol),
                     reads=[acc2[:], cbcol], writes=[xcT[:]])
                mk.dma("gpsimd", xcT_o[d][dt * 128:(dt + 1) * 128, :], xcT[:])
                for (lw, wid, accp, pps) in ((xpd, 64, dtraw, pd64), (xpb, 16, btacc, pd16),
                                             (xpc, 16, ctacc, pd16)):
                    for c0 in range(0, 2048, 512):
                        lhs = lw[:, dt * wid:(dt + 1) * wid]
                        rhs = xcT[:, c0:c0 + 512]
                        mk.t(lambda E, lhs=lhs, rhs=rhs, pps=pps, wid=wid: E.matmul(
                                pps[0:wid, 0:512], lhs, rhs, start=True, stop=True),
                             reads=[lhs, rhs], writes=[pps[0:wid, 0:512]])
                        aap = accp[:, c0:c0 + 512]
                        mk.v(lambda E, aap=aap, pps=pps, wid=wid: E.tensor_add(aap, aap, pps[0:wid, 0:512]),
                             reads=[aap, pps[0:wid, 0:512]], writes=[aap])
                for c0 in range(0, 2048, 512):
                    u_mm(mk, ph[:, 0:512],
                         [iwz[:, k * 2048 + dt * 128:k * 2048 + dt * 128 + 128] for k in range(8)],
                         [XfE[:, k * 2176 + 64 + c0:k * 2176 + 64 + c0 + 512] for k in range(8)])
                    dst = dtxt[:, c0:c0 + 512]
                    mk.s(lambda E, dst=dst: E.activation(out=dst, in_=ph[:, 0:512], func=AF.Silu),
                         reads=[ph[:, 0:512]], writes=[dst])
                mk.dma("gpsimd", szT_o[d][dt * 128:(dt + 1) * 128, :], dtxt[:])
            mk.dma("gpsimd", bt_o[d], btacc[:])
            mk.dma("gpsimd", ct_o[d], ctacc[:])
            mk.v(lambda E: E.tensor_copy(dtrawb[:], dtraw[:]), reads=[dtraw[:]], writes=[dtrawb[:]])
            for dt in range(16):
                for c0 in range(0, 2048, 512):
                    lhs = dtw[:, dt * 128:(dt + 1) * 128]
                    rhs = dtrawb[:, c0:c0 + 512]
                    mk.t(lambda E, lhs=lhs, rhs=rhs: E.matmul(ph[:, 0:512], lhs, rhs,
                                                              start=True, stop=True),
                         reads=[lhs, rhs], writes=[ph[:, 0:512]])
                    dst = dtTt[:, c0:c0 + 512]
                    mk.v(lambda E, dst=dst: E.tensor_copy(dst, ph[:, 0:512]),
                         reads=[ph[:, 0:512]], writes=[dst])
                bcol = dtbs[:, d * 16 + dt:d * 16 + dt + 1]
                mk.s(lambda E, bcol=bcol: E.activation(out=sp1[:], in_=dtTt[:], func=AF.Abs, bias=bcol),
                     reads=[dtTt[:], bcol], writes=[sp1[:]])
                mk.s(lambda E: E.activation(out=sp1[:], in_=sp1[:], func=AF.Exp, scale=-1.0),
                     reads=[sp1[:]], writes=[sp1[:]])
                mk.s(lambda E: E.activation(out=sp1[:], in_=sp1[:], func=AF.Ln, bias=one128[:]),
                     reads=[sp1[:], one128[:]], writes=[sp1[:]])
                mk.s(lambda E, bcol=bcol: E.activation(out=dtTt[:], in_=dtTt[:], func=AF.Relu, bias=bcol),
                     reads=[dtTt[:], bcol], writes=[dtTt[:]])
                mk.v(lambda E: E.tensor_add(dtTt[:], dtTt[:], sp1[:]), reads=[dtTt[:], sp1[:]], writes=[dtTt[:]])
                mk.dma("gpsimd", dtT_o[d][dt * 128:(dt + 1) * 128, :], dtTt[:])
                mk.dma("sync", xcT[:], xcT_o[d][dt * 128:(dt + 1) * 128, :], chan=1)
                mk.v(lambda E: E.tensor_mul(dtxt[:], dtTt[:], xcT[:]), reads=[dtTt[:], xcT[:]], writes=[dtxt[:]])
                mk.dma("gpsimd", dtxT_o[d][dt * 128:(dt + 1) * 128, :], dtxt[:])
        mk.emit()
    return nc


def mamba_a_inmaps(x3_full, inw, cw, cb, xpw, dtwm, dtbm, consts):
    ins = []
    cwf = np.zeros((128, 128), np.float32)
    cbs = np.zeros((128, 32), np.float32)
    dtbs = np.zeros((128, 32), np.float32)
    for d in range(2):
        for dt in range(16):
            cbs[:, d * 16 + dt] = cb[d, dt * 128:(dt + 1) * 128]
            dtbs[:, d * 16 + dt] = dtbm[d, dt * 128:(dt + 1) * 128]
            for j in range(4):
                cwf[:, (d * 16 + dt) * 4 + j] = cw[d, dt * 128:(dt + 1) * 128, j]
    for c in range(8):
        b, h = c // 2, c % 2
        r0 = h * 2048
        xe = np.zeros((2176, 1024), np.float32)
        lo, hi = r0 - 64, r0 + 2112
        slo, shi = max(lo, 0), min(hi, 4096)
        xe[slo - lo:shi - lo] = x3_full[b, slo:shi]
        m = dict(consts)
        m["xe"] = xe
        for d in range(2):
            m[f"iwx{d}"] = np.ascontiguousarray(inw[d][:, 0:2048])
            m[f"iwz{d}"] = np.ascontiguousarray(inw[d][:, 2048:4096])
            m[f"xpd{d}"] = np.ascontiguousarray(xpw[d][:, 0:64])
            m[f"xpb{d}"] = np.ascontiguousarray(xpw[d][:, 64:80])
            m[f"xpc{d}"] = np.ascontiguousarray(xpw[d][:, 80:96])
            m[f"dtw{d}"] = np.ascontiguousarray(dtwm[d])
        m["cw"] = np.zeros((128, 8), np.float32)
        m["cwf"] = cwf; m["cb"] = cbs; m["dtb"] = dtbs
        ins.append(m)
    return ins


def build_mamba_b():
    nc = bass.Bass("TRN2", target_bir_lowering=False)
    dtT_d = nc.dram_tensor("dtT", [2048, 4096], F32, kind="ExternalInput").ap()
    dtx_d = nc.dram_tensor("dtx", [2048, 4096], BF16, kind="ExternalInput").ap()
    bt_d = nc.dram_tensor("bt", [16, 4096], F32, kind="ExternalInput").ap()
    ct_d = nc.dram_tensor("ct", [16, 4096], F32, kind="ExternalInput").ap()
    alog_d = nc.dram_tensor("alog", [2048, 16], F32, kind="ExternalInput").ap()
    identb_d, onesf_d = std_consts(nc)
    yT_o = nc.dram_tensor("yT", [2048, 4096], F32, kind="ExternalOutput").ap()

    mk = MK(nc)
    es = contextlib.ExitStack()
    with es:
        identb, onesf = load_consts(mk, es, nc, identb_d, onesf_d)
        acol = alloc(es, nc, "acol", [128, 256])
        ast = alloc(es, nc, "ast", [128, 16])
        for dt in range(16):
            mk.dma("sync", ast[:], alog_d[dt * 128:(dt + 1) * 128, :])
            dst = acol[:, dt * 16:(dt + 1) * 16]
            mk.s(lambda E, dst=dst: E.activation(out=dst, in_=ast[:], func=AF.Exp),
                 reads=[ast[:]], writes=[dst])
            mk.v(lambda E, dst=dst: E.tensor_single_scalar(dst, dst, -1.0, ALU.mult),
                 reads=[dst], writes=[dst])
        brow = alloc(es, nc, "brow", [1, 512])
        crow = brow
        Bbc = alloc(es, nc, "Bbc", [128, 4096], BF16)
        Cbc = alloc(es, nc, "Cbc", [128, 4096], BF16)
        psb = alloc(es, nc, "psb", [128, 512], psum=True)
        G = 2
        dtt = [alloc(es, nc, f"dtt{j}", [128, 4096]) for j in range(G)]
        dxt = [alloc(es, nc, f"dxt{j}", [128, 4096], BF16) for j in range(G)]
        yac = [alloc(es, nc, f"yac{j}", [128, 4096]) for j in range(G)]
        dA = alloc(es, nc, "dA", [128, 4096])
        uu = alloc(es, nc, "uu", [128, 4096], BF16)
        hh = alloc(es, nc, "hh", [128, 4096])

        for half in range(2):
            for dtg in range(8):
                for j in range(G):
                    dt = dtg * G + j
                    mk.dma("sync", dtt[j][:], dtT_d[dt * 128:(dt + 1) * 128, :], chan=1)
                    mk.dma("sync", dxt[j][:], dtx_d[dt * 128:(dt + 1) * 128, :], chan=1)
                    if half == 0:
                        mk.v(lambda E, j=j: E.memset(yac[j][:], 0.0), writes=[yac[j][:]])
                    else:
                        mk.dma("sync", yac[j][:], yT_o[dt * 128:(dt + 1) * 128, :], chan=1)
                for n in range(half * 8, half * 8 + 8):
                    for c0 in range(0, 4096, 512):
                        mk.dma("sync", brow[:], bt_d[n:n + 1, c0:c0 + 512])
                        u_bcast_row(mk, brow[:], Bbc[:, c0:c0 + 512], onesf[:], psb[:], 512)
                        mk.dma("sync", crow[:], ct_d[n:n + 1, c0:c0 + 512])
                        u_bcast_row(mk, crow[:], Cbc[:, c0:c0 + 512], onesf[:], psb[:], 512)
                    for j in range(G):
                        dt = dtg * G + j
                        ac = acol[:, dt * 16 + n:dt * 16 + n + 1]
                        mk.s(lambda E, j=j, ac=ac: E.activation(out=dA[:], in_=dtt[j][:], func=AF.Exp,
                                                                scale=ac),
                             reads=[dtt[j][:], ac], writes=[dA[:]])
                        mk.v(lambda E, j=j: E.tensor_mul(uu[:], dxt[j][:], Bbc[:]),
                             reads=[dxt[j][:], Bbc[:]], writes=[uu[:]])
                        mk.v(lambda E: E.tensor_tensor_scan(hh[:], dA[:], uu[:], 0.0, ALU.mult, ALU.add),
                             reads=[dA[:], uu[:]], writes=[hh[:]])
                        mk.v(lambda E: E.tensor_mul(dA[:], hh[:], Cbc[:]), reads=[hh[:], Cbc[:]], writes=[dA[:]])
                        mk.v(lambda E, j=j: E.tensor_add(yac[j][:], yac[j][:], dA[:]),
                             reads=[yac[j][:], dA[:]], writes=[yac[j][:]])
                for j in range(G):
                    dt = dtg * G + j
                    mk.dma("gpsimd", yT_o[dt * 128:(dt + 1) * 128, :], yac[j][:])
        mk.emit()
    return nc


def build_mamba_c(dbg=""):
    nc = bass.Bass("TRN2", target_bir_lowering=False)
    x3_d = nc.dram_tensor("x3", [2048, 1024], F32, kind="ExternalInput").ap()
    yT_d = [nc.dram_tensor(f"yTs{d}", [2048, 2048], F32, kind="ExternalInput").ap() for d in range(2)]
    xcT_d = [nc.dram_tensor(f"xcTs{d}", [2048, 2048], BF16, kind="ExternalInput").ap() for d in range(2)]
    szT_d = [nc.dram_tensor(f"szTs{d}", [2048, 2048], BF16, kind="ExternalInput").ap() for d in range(2)]
    dcols_d = nc.dram_tensor("dcols", [128, 32], F32, kind="ExternalInput").ap()
    owT_d = [nc.dram_tensor(f"owT{d}", [1024, 2048], F32, kind="ExternalInput").ap() for d in range(2)]
    pw_d = [nc.dram_tensor(f"pw{d}", [1024, 1024], F32, kind="ExternalInput").ap() for d in range(2)]
    pb_d = nc.dram_tensor("pb", [1, 1024], F32, kind="ExternalInput").ap()
    identb_d, onesf_d = std_consts(nc)
    xo_d = nc.dram_tensor("xo", [2048, 1024], F32, kind="ExternalOutput").ap()

    mk = MK(nc)
    es = contextlib.ExitStack()
    with es:
        identb, onesf = load_consts(mk, es, nc, identb_d, onesf_d)
        wst = alloc(es, nc, "wst", [128, 2048])
        owT = alloc(es, nc, "owT_s", [128, 8 * 2048], BF16)
        pw = alloc(es, nc, "pw_s", [128, 8 * 1024], BF16)
        wcomb = [alloc(es, nc, f"wcomb{d}", [128, 16 * 1024], BF16) for d in range(2)]
        ph = alloc(es, nc, "ph", [128, 512], psum=True)
        ph2 = alloc(es, nc, "ph2", [128, 512], psum=True)
        ph3 = alloc(es, nc, "ph3", [128, 512], psum=True)
        for d in range(2):
            for k in range(8):
                st = wst[:, 0:2048]
                mk.dma("sync", st, owT_d[d][k * 128:(k + 1) * 128, :])
                dst = owT[:, k * 2048:(k + 1) * 2048]
                mk.v(lambda E, st=st, dst=dst: E.tensor_copy(dst, st), reads=[st], writes=[dst])
                st2 = wst[:, 0:1024]
                mk.dma("sync", st2, pw_d[d][k * 128:(k + 1) * 128, :])
                dst2 = pw[:, k * 1024:(k + 1) * 1024]
                mk.v(lambda E, st2=st2, dst2=dst2: E.tensor_copy(dst2, st2), reads=[st2], writes=[dst2])
            for m in range(16):
                for c0 in range(0, 1024, 512):
                    u_mm(mk, ph[:, 0:512],
                         [owT[:, k * 2048 + m * 128:k * 2048 + (m + 1) * 128] for k in range(8)],
                         [pw[:, k * 1024 + c0:k * 1024 + c0 + 512] for k in range(8)])
                    dst = wcomb[d][:, m * 1024 + c0:m * 1024 + c0 + 512]
                    mk.v(lambda E, dst=dst: E.tensor_copy(dst, ph[:, 0:512]),
                         reads=[ph[:, 0:512]], writes=[dst])
        dcols = alloc(es, nc, "dcols_s", [128, 32])
        mk.dma("sync", dcols[:], dcols_d)
        pbr = alloc(es, nc, "pbr", [1, 1024])
        mk.dma("sync", pbr[:], pb_d)
        pb_bc = alloc(es, nc, "pb_bc", [128, 1024])
        u_bcast_row(mk, pbr[:], pb_bc[:], onesf[:], ph[:], 1024)
        yt = alloc(es, nc, "yt", [128, 128])
        xct = alloc(es, nc, "xct", [128, 128], BF16)
        szt = alloc(es, nc, "szt", [128, 128], BF16)
        gT = alloc(es, nc, "gT", [128, 32 * 128], BF16)
        xres = alloc(es, nc, "xres", [128, 1024])
        outsb = alloc(es, nc, "outsb", [128, 1024])

        for i in range(16):
            r0 = i * 128
            mk.dma("sync", xres[:], x3_d[r0:r0 + 128, :], chan=1)
            for d in range(2):
                for dt in range(16):
                    mk.dma("sync", yt[:], yT_d[d][dt * 128:(dt + 1) * 128, r0:r0 + 128], chan=1)
                    mk.dma("sync", xct[:], xcT_d[d][dt * 128:(dt + 1) * 128, r0:r0 + 128], chan=1)
                    mk.dma("sync", szt[:], szT_d[d][dt * 128:(dt + 1) * 128, r0:r0 + 128], chan=1)
                    dc = dcols[:, d * 16 + dt:d * 16 + dt + 1]
                    slot = gT[:, (d * 16 + dt) * 128:(d * 16 + dt + 1) * 128]
                    mk.v(lambda E, dc=dc, slot=slot: E.scalar_tensor_tensor(yt[:], xct[:], dc, yt[:],
                                                                            ALU.mult, ALU.add),
                         reads=[xct[:], dc, yt[:]], writes=[yt[:]])
                    mk.v(lambda E, slot=slot: E.tensor_tensor(slot, yt[:], szt[:], ALU.mult),
                         reads=[yt[:], szt[:]], writes=[slot])
            srange = list(range(16)) if "onedir" in dbg else list(range(32))
            for c0 in range(0, 1024, 512):
                pcur = ph2 if c0 == 0 else ph3
                u_mm(mk, pcur[:, 0:512],
                     [gT[:, s * 128:(s + 1) * 128] for s in srange],
                     [wcomb[s // 16][:, (s % 16) * 1024 + c0:(s % 16) * 1024 + c0 + 512]
                      for s in srange])
                oap = outsb[:, c0:c0 + 512]
                if "nopb" in dbg:
                    mk.v(lambda E, oap=oap, pcur=pcur: E.tensor_copy(oap, pcur[:, 0:512]),
                         reads=[pcur[:, 0:512]], writes=[oap])
                else:
                    mk.v(lambda E, oap=oap, pcur=pcur, c0=c0: E.tensor_add(oap, pcur[:, 0:512],
                                                                           pb_bc[:, c0:c0 + 512]),
                         reads=[pcur[:, 0:512], pb_bc[:, c0:c0 + 512]], writes=[oap])
            if "noxres" not in dbg:
                mk.v(lambda E: E.tensor_add(outsb[:], outsb[:], xres[:]), reads=[outsb[:], xres[:]], writes=[outsb[:]])
            mk.dma("gpsimd", xo_d[r0:r0 + 128, :], outsb[:])
        mk.emit()
    return nc



def build_moe_route(has_p):
    nc = bass.Bass("TRN2", target_bir_lowering=False)
    xb_d = nc.dram_tensor("xb", [2048, 1024], F32, kind="ExternalInput").ap()
    if has_p:
        p0_d = nc.dram_tensor("p0", [2048, 1024], F32, kind="ExternalInput").ap()
        p1_d = nc.dram_tensor("p1", [2048, 1024], F32, kind="ExternalInput").ap()
    wr_d = nc.dram_tensor("wr", [1024, 8], F32, kind="ExternalInput").ap()
    iota8r_d = nc.dram_tensor("iota8r", [128, 8], F32, kind="ExternalInput").ap()
    eids_d = nc.dram_tensor("eids", [8, 1], F32, kind="ExternalInput").ap()
    identb_d, onesf_d = std_consts(nc)
    xc_o = nc.dram_tensor("xc_o", [2048, 1024], F32, kind="ExternalOutput").ap()
    xn_o = nc.dram_tensor("xn_o", [2048, 1024], BF16, kind="ExternalOutput").ap()
    idx_o = nc.dram_tensor("idx_o", [2048, 1], F32, kind="ExternalOutput").ap()
    pmax_o = nc.dram_tensor("pmax_o", [2048, 1], F32, kind="ExternalOutput").ap()
    pos_o = nc.dram_tensor("pos_o", [1, 2048], F32, kind="ExternalOutput").ap()
    cnt_o = nc.dram_tensor("cnt_o", [8, 1], F32, kind="ExternalOutput").ap()
    idx_sc = nc.dram_tensor("idx_sc", [2048, 1], F32, kind="Internal").ap()

    mk = MK(nc)
    es = contextlib.ExitStack()
    with es:
        identb, onesf = load_consts(mk, es, nc, identb_d, onesf_d)
        eps128 = alloc(es, nc, "eps128", [128, 1])
        mk.v(lambda E: E.memset(eps128[:], 1e-5), writes=[eps128[:]])
        wst = alloc(es, nc, "wst", [128, 8])
        wr = alloc(es, nc, "wr_s", [128, 8 * 8], BF16)
        for k in range(8):
            mk.dma("sync", wst[:], wr_d[k * 128:(k + 1) * 128, :])
            dst = wr[:, k * 8:(k + 1) * 8]
            mk.v(lambda E, dst=dst: E.tensor_copy(dst, wst[:]), reads=[wst[:]], writes=[dst])
        iota8 = alloc(es, nc, "iota8_s", [128, 8])
        mk.dma("sync", iota8[:], iota8r_d)
        xt = alloc(es, nc, "xt", [128, 1024])
        pb = alloc(es, nc, "pb", [128, 1024])
        normed = alloc(es, nc, "normed", [128, 1024], BF16)
        stats = alloc(es, nc, "stats", [128, 8])
        xf = alloc(es, nc, "xf", [128, 1024], BF16)
        lg = alloc(es, nc, "lg", [128, 8])
        eq = alloc(es, nc, "eq", [128, 8])
        col = alloc(es, nc, "col", [128, 4])
        psT = alloc(es, nc, "psT", [128, 1024], BF16, psum=True)
        ph = alloc(es, nc, "ph", [128, 512], psum=True)
        for i in range(16):
            r0 = i * 128
            mk.dma("sync", xt[:], xb_d[r0:r0 + 128, :], chan=1)
            if has_p:
                mk.dma("sync", pb[:], p0_d[r0:r0 + 128, :], chan=1)
                mk.v(lambda E: E.tensor_add(xt[:], xt[:], pb[:]), reads=[xt[:], pb[:]], writes=[xt[:]])
                mk.dma("sync", pb[:], p1_d[r0:r0 + 128, :], chan=1)
                mk.v(lambda E: E.tensor_add(xt[:], xt[:], pb[:]), reads=[xt[:], pb[:]], writes=[xt[:]])
            mk.dma("gpsimd", xc_o[r0:r0 + 128, :], xt[:])
            u_ln(mk, xt[:], normed[:], stats[:, 0:4], 1024, eps_ap=eps128[:], sq_ap=pb[:])
            mk.dma("gpsimd", xn_o[r0:r0 + 128, :], normed[:])
            u_transpose(mk, lambda j: normed[:, j * 128:(j + 1) * 128],
                        lambda j: xf[:, j * 128:(j + 1) * 128], 8, identb[:], psT)
            u_mm(mk, ph[:, 0:8],
                 [xf[:, k * 128:(k + 1) * 128] for k in range(8)],
                 [wr[:, k * 8:(k + 1) * 8] for k in range(8)])
            mk.v(lambda E: E.tensor_copy(lg[:], ph[:, 0:8]), reads=[ph[:, 0:8]], writes=[lg[:]])
            mk.v(lambda E: E.reduce_max(col[:, 0:1], lg[:], AX.X), reads=[lg[:]], writes=[col[:, 0:1]])
            mk.v(lambda E: E.tensor_single_scalar(col[:, 1:2], col[:, 0:1], -1.0, ALU.mult),
                 reads=[col[:, 0:1]], writes=[col[:, 1:2]])
            mk.s(lambda E: E.activation(out=eq[:], in_=lg[:], func=AF.Exp, bias=col[:, 1:2],
                                        accum_out=col[:, 2:3]),
                 reads=[lg[:], col[:, 1:2]], writes=[eq[:], col[:, 2:3]])
            mk.v(lambda E: E.reciprocal(col[:, 3:4], col[:, 2:3]), reads=[col[:, 2:3]], writes=[col[:, 3:4]])
            mk.dma("gpsimd", pmax_o[r0:r0 + 128, :], col[:, 3:4])
            mk.v(lambda E: E.tensor_scalar(eq[:], lg[:], col[:, 0:1], None, ALU.is_ge),
                 reads=[lg[:], col[:, 0:1]], writes=[eq[:]])
            mk.v(lambda E: E.scalar_tensor_tensor(eq[:], eq[:], -1000.0, iota8[:], ALU.mult, ALU.add),
                 reads=[eq[:], iota8[:]], writes=[eq[:]])
            mk.v(lambda E: E.tensor_reduce(col[:, 0:1], eq[:], AX.X, ALU.min),
                 reads=[eq[:]], writes=[col[:, 0:1]])
            mk.v(lambda E: E.tensor_single_scalar(col[:, 0:1], col[:, 0:1], 1000.0, ALU.add),
                 reads=[col[:, 0:1]], writes=[col[:, 0:1]])
            mk.dma("gpsimd", idx_o[r0:r0 + 128, :], col[:, 0:1])
            mk.dma("gpsimd", idx_sc[r0:r0 + 128, :], col[:, 0:1])
        idxrow = alloc(es, nc, "idxrow", [1, 2048])
        mk.dma("sync", idxrow[:], idx_sc.rearrange("(a b) c -> a (b c)", a=1))
        idxb = alloc(es, nc, "idxb", [8, 2048])
        ones8 = alloc(es, nc, "ones8", [8, 2048])
        oh = alloc(es, nc, "oh", [8, 2048])
        rk = alloc(es, nc, "rk", [8, 2048])
        eidc = alloc(es, nc, "eidc", [8, 1])
        mk.dma("sync", eidc[:], eids_d)
        mk.v(lambda E: E.memset(ones8[:], 1.0), writes=[ones8[:]])
        for c0 in range(0, 2048, 512):
            dst = ph[0:8, 0:512]
            src = idxrow[:, c0:c0 + 512]
            mk.t(lambda E, dst=dst, src=src: E.matmul(dst, onesf[:, 0:8], src, start=True, stop=True),
                 reads=[onesf[:, 0:8], src], writes=[dst])
            dap = idxb[:, c0:c0 + 512]
            mk.v(lambda E, dap=dap, dst=dst: E.tensor_copy(dap, dst), reads=[dst], writes=[dap])
        mk.v(lambda E: E.tensor_scalar(oh[:], idxb[:], eidc[:], None, ALU.is_equal),
             reads=[idxb[:], eidc[:]], writes=[oh[:]])
        mk.v(lambda E: E.tensor_tensor_scan(rk[:], ones8[:], oh[:], 0.0, ALU.mult, ALU.add),
             reads=[ones8[:], oh[:]], writes=[rk[:]])
        mk.dma("gpsimd", cnt_o, rk[:, 2047:2048])
        mk.v(lambda E: E.tensor_mul(oh[:], oh[:], rk[:]), reads=[oh[:], rk[:]], writes=[oh[:]])
        one8c = alloc(es, nc, "one8c", [8, 1])
        mk.v(lambda E: E.memset(one8c[:], 1.0), writes=[one8c[:]])
        prow = alloc(es, nc, "prow", [1, 2048])
        for c0 in range(0, 2048, 512):
            dst = ph[0:1, 0:512]
            src = oh[:, c0:c0 + 512]
            mk.t(lambda E, dst=dst, src=src: E.matmul(dst, one8c[:], src, start=True, stop=True),
                 reads=[one8c[:], src], writes=[dst])
            dap = prow[:, c0:c0 + 512]
            mk.v(lambda E, dap=dap, dst=dst: E.tensor_single_scalar(dap, dst, -1.0, ALU.add),
                 reads=[dst], writes=[dap])
        mk.dma("gpsimd", pos_o, prow[:])
        mk.emit()
    return nc


def build_moe_expert():
    nc = bass.Bass("TRN2", target_bir_lowering=False)
    xn_d = nc.dram_tensor("xn", [16384, 1024], BF16, kind="ExternalInput").ap()
    idx_d = nc.dram_tensor("idx", [16384, 1], F32, kind="ExternalInput").ap()
    pos_d = nc.dram_tensor("pos", [16384, 1], F32, kind="ExternalInput").ap()
    cntT_d = nc.dram_tensor("cntT", [8, 8], F32, kind="ExternalInput").ap()
    w1_d = nc.dram_tensor("w1", [1024, 4096], F32, kind="ExternalInput").ap()
    w2_d = nc.dram_tensor("w2", [4096, 1024], F32, kind="ExternalInput").ap()
    eid_d = nc.dram_tensor("eid", [1, 1], F32, kind="ExternalInput").ap()
    iota8r_d = nc.dram_tensor("iota8r", [128, 8], F32, kind="ExternalInput").ap()
    tokid_d = nc.dram_tensor("tokid", [16384, 8], F32, kind="ExternalInput").ap()
    identb_d, onesf_d = std_consts(nc)
    eout_o = nc.dram_tensor("eout", [2560, 1024], F32, kind="ExternalOutput").ap()
    inv_sc = nc.dram_tensor("inv_sc", [2561, 8], F32, kind="Internal").ap()

    from concourse.bass_core_rust import IndirectOffsetOnAxis
    mk = MK(nc)
    es = contextlib.ExitStack()
    with es:
        identb, onesf = load_consts(mk, es, nc, identb_d, onesf_d)
        wst = alloc(es, nc, "wst", [128, 4096])
        w1 = alloc(es, nc, "w1_s", [128, 8 * 4096], BF16)
        w2 = alloc(es, nc, "w2_s", [128, 32 * 1024], BF16)
        for k in range(8):
            st = wst[:, 0:4096]
            mk.dma("sync", st, w1_d[k * 128:(k + 1) * 128, :])
            dst = w1[:, k * 4096:(k + 1) * 4096]
            mk.v(lambda E, st=st, dst=dst: E.tensor_copy(dst, st), reads=[st], writes=[dst])
        for k in range(32):
            st = wst[:, 0:1024]
            mk.dma("sync", st, w2_d[k * 128:(k + 1) * 128, :])
            dst = w2[:, k * 1024:(k + 1) * 1024]
            mk.v(lambda E, st=st, dst=dst: E.tensor_copy(dst, st), reads=[st], writes=[dst])
        cnt = alloc(es, nc, "cnt", [8, 8])
        rkx = alloc(es, nc, "rkx", [8, 8])
        on8 = alloc(es, nc, "on8", [8, 8])
        mk.dma("sync", cnt[:], cntT_d)
        mk.v(lambda E: E.memset(on8[:], 1.0), writes=[on8[:]])
        mk.v(lambda E: E.tensor_tensor_scan(rkx[:], on8[:], cnt[:], 0.0, ALU.mult, ALU.add),
             reads=[on8[:], cnt[:]], writes=[rkx[:]])
        mk.v(lambda E: E.tensor_sub(rkx[:], rkx[:], cnt[:]), reads=[rkx[:], cnt[:]], writes=[rkx[:]])
        psT = alloc(es, nc, "psT", [128, 1024], BF16, psum=True)
        ph = alloc(es, nc, "ph", [128, 512], psum=True)
        idf8 = alloc(es, nc, "idf8", [8, 8])
        mk.v(lambda E: E.tensor_copy(idf8[:], identb[0:8, 0:8]), reads=[identb[0:8, 0:8]], writes=[idf8[:]])
        pt8 = alloc(es, nc, "pt8", [8, 8], psum=True)
        exT = alloc(es, nc, "exT", [8, 8])
        mk.t(lambda E: E.matmul(pt8[:], rkx[:], idf8[:], start=True, stop=True),
             reads=[rkx[:], idf8[:]], writes=[pt8[:]])
        mk.v(lambda E: E.tensor_copy(exT[:], pt8[:]), reads=[pt8[:]], writes=[exT[:]])
        exb = alloc(es, nc, "exb", [128, 8 * 8])
        rowtmp = alloc(es, nc, "rowtmp", [1, 8])
        for c in range(8):
            mk.dma("sync", rowtmp[:], exT[c:c + 1, :])
            u_bcast_row(mk, rowtmp[:], exb[:, c * 8:(c + 1) * 8], onesf[:], ph[:], 8)
        eid128 = alloc(es, nc, "eid128", [128, 1])
        est = alloc(es, nc, "est", [1, 1])
        mk.dma("sync", est[:], eid_d)
        u_bcast_row(mk, est[:], eid128[:], onesf[:], ph[:], 1)
        iota8 = alloc(es, nc, "iota8_s", [128, 8])
        mk.dma("sync", iota8[:], iota8r_d)

        zcol = alloc(es, nc, "zcol", [128, 8])
        mk.v(lambda E: E.memset(zcol[:], 0.0), writes=[zcol[:]])
        for t in range(20):
            mk.dma("gpsimd", inv_sc[t * 128:(t + 1) * 128, :], zcol[:])
        mk.dma("gpsimd", inv_sc[2560:2561, :], zcol[0:1, :])
        idxt = alloc(es, nc, "idxt", [128, 1])
        post = alloc(es, nc, "post", [128, 1])
        tkt = alloc(es, nc, "tkt", [128, 8])
        eqx = alloc(es, nc, "eqx", [128, 8])
        fm = alloc(es, nc, "fm", [128, 3])
        slot = alloc(es, nc, "slot", [128, 1])
        slot_i = alloc(es, nc, "slot_i", [128, 1], I32)
        for t in range(128):
            r0 = t * 128
            ch = r0 // 2048
            mk.dma("sync", idxt[:], idx_d[r0:r0 + 128, :], chan=1)
            mk.dma("sync", post[:], pos_d[r0:r0 + 128, :], chan=1)
            mk.dma("sync", tkt[:], tokid_d[r0:r0 + 128, :], chan=1)
            mk.v(lambda E: E.tensor_scalar(eqx[:], iota8[:], idxt[:], None, ALU.is_equal),
                 reads=[iota8[:], idxt[:]], writes=[eqx[:]])
            exc = exb[:, ch * 8:(ch + 1) * 8]
            mk.v(lambda E, exc=exc: E.tensor_mul(eqx[:], eqx[:], exc), reads=[eqx[:], exc], writes=[eqx[:]])
            mk.v(lambda E: E.reduce_sum(fm[:, 0:1], eqx[:], AX.X), reads=[eqx[:]], writes=[fm[:, 0:1]])
            mk.v(lambda E: E.tensor_add(post[:], post[:], fm[:, 0:1]), reads=[post[:], fm[:, 0:1]], writes=[post[:]])
            mk.v(lambda E: E.tensor_scalar(fm[:, 1:2], idxt[:], eid128[:], None, ALU.is_equal),
                 reads=[idxt[:], eid128[:]], writes=[fm[:, 1:2]])
            mk.v(lambda E: E.tensor_single_scalar(fm[:, 2:3], post[:], 2560.0, ALU.is_lt),
                 reads=[post[:]], writes=[fm[:, 2:3]])
            mk.v(lambda E: E.tensor_mul(fm[:, 1:2], fm[:, 1:2], fm[:, 2:3]),
                 reads=[fm[:, 1:3]], writes=[fm[:, 1:2]])
            mk.v(lambda E: E.tensor_single_scalar(slot[:], post[:], -2560.0, ALU.add),
                 reads=[post[:]], writes=[slot[:]])
            mk.v(lambda E: E.tensor_scalar(slot[:], slot[:], fm[:, 1:2], 2560.0, ALU.mult, ALU.add),
                 reads=[slot[:], fm[:, 1:2]], writes=[slot[:]])
            mk.v(lambda E: E.tensor_copy(slot_i[:], slot[:]), reads=[slot[:]], writes=[slot_i[:]])
            mk.op("gpsimd", lambda E: E.indirect_dma_start(
                    inv_sc, IndirectOffsetOnAxis(slot_i[:], 0), tkt[:], None),
                  reads=[tkt[:], slot_i[:]], writes=[inv_sc], dma=True, chan=1)
        invt = alloc(es, nc, "invt", [128, 8])
        inv_i = alloc(es, nc, "inv_i", [128, 1], I32)
        xg = alloc(es, nc, "xg", [128, 1024], BF16)
        xgf = alloc(es, nc, "xgf", [128, 1024], BF16)
        hb = alloc(es, nc, "hb", [128, 4096], BF16)
        hf = alloc(es, nc, "hf", [128, 4096], BF16)
        outb = alloc(es, nc, "outb", [128, 1024])
        for t in range(20):
            r0 = t * 128
            mk.dma("sync", invt[:], inv_sc[r0:r0 + 128, :], chan=1)
            mk.v(lambda E: E.tensor_copy(inv_i[:], invt[:, 0:1]), reads=[invt[:, 0:1]], writes=[inv_i[:]])
            mk.op("gpsimd", lambda E: E.indirect_dma_start(
                    xg[:], None, xn_d, IndirectOffsetOnAxis(inv_i[:], 0),
                    bounds_check=16384, oob_is_err=False),
                  reads=[xn_d, inv_i[:]], writes=[xg[:]], dma=True, chan=2)
            u_transpose(mk, lambda j: xg[:, j * 128:(j + 1) * 128],
                        lambda j: xgf[:, j * 128:(j + 1) * 128], 8, identb[:], psT)
            for c in range(8):
                u_mm(mk, ph[:, 0:512],
                     [xgf[:, k * 128:(k + 1) * 128] for k in range(8)],
                     [w1[:, k * 4096 + c * 512:k * 4096 + (c + 1) * 512] for k in range(8)])
                dst = hb[:, c * 512:(c + 1) * 512]
                mk.s(lambda E, dst=dst: E.activation(out=dst, in_=ph[:, 0:512], func=AF.Relu),
                     reads=[ph[:, 0:512]], writes=[dst])
            u_transpose(mk, lambda j: hb[:, j * 128:(j + 1) * 128],
                        lambda j: hf[:, j * 128:(j + 1) * 128], 32, identb[:], psT)
            for c in range(2):
                u_mm(mk, ph[:, 0:512],
                     [hf[:, k * 128:(k + 1) * 128] for k in range(32)],
                     [w2[:, k * 1024 + c * 512:k * 1024 + (c + 1) * 512] for k in range(32)])
                oap = outb[:, c * 512:(c + 1) * 512]
                mk.v(lambda E, oap=oap: E.tensor_copy(oap, ph[:, 0:512]),
                     reads=[ph[:, 0:512]], writes=[oap])
            mk.dma("gpsimd", eout_o[r0:r0 + 128, :], outb[:])
        mk.emit()
    return nc


def build_moe_combine():
    nc = bass.Bass("TRN2", target_bir_lowering=False)
    xc_d = nc.dram_tensor("xc", [2048, 1024], F32, kind="ExternalInput").ap()
    xn_d = nc.dram_tensor("xn", [2048, 1024], BF16, kind="ExternalInput").ap()
    eout_d = nc.dram_tensor("eoutx", [20481, 1024], F32, kind="ExternalInput").ap()
    idx_d = nc.dram_tensor("idx", [2048, 1], F32, kind="ExternalInput").ap()
    pos_d = nc.dram_tensor("pos", [2048, 1], F32, kind="ExternalInput").ap()
    pmax_d = nc.dram_tensor("pmax", [2048, 1], F32, kind="ExternalInput").ap()
    cntT_d = nc.dram_tensor("cntT", [8, 8], F32, kind="ExternalInput").ap()
    chid_d = nc.dram_tensor("chid", [1, 1], F32, kind="ExternalInput").ap()
    iota8r_d = nc.dram_tensor("iota8r", [128, 8], F32, kind="ExternalInput").ap()
    eids_d = nc.dram_tensor("eids", [8, 1], F32, kind="ExternalInput").ap()
    identb_d, onesf_d = std_consts(nc)
    xo_d = nc.dram_tensor("xo", [2048, 1024], F32, kind="ExternalOutput").ap()

    from concourse.bass_core_rust import IndirectOffsetOnAxis
    mk = MK(nc)
    es = contextlib.ExitStack()
    with es:
        identb, onesf = load_consts(mk, es, nc, identb_d, onesf_d)
        ph = alloc(es, nc, "ph", [128, 512], psum=True)
        cnt = alloc(es, nc, "cnt", [8, 8])
        rkx = alloc(es, nc, "rkx", [8, 8])
        on8 = alloc(es, nc, "on8", [8, 8])
        mk.dma("sync", cnt[:], cntT_d)
        mk.v(lambda E: E.memset(on8[:], 1.0), writes=[on8[:]])
        mk.v(lambda E: E.tensor_tensor_scan(rkx[:], on8[:], cnt[:], 0.0, ALU.mult, ALU.add),
             reads=[on8[:], cnt[:]], writes=[rkx[:]])
        mk.v(lambda E: E.tensor_sub(rkx[:], rkx[:], cnt[:]), reads=[rkx[:], cnt[:]], writes=[rkx[:]])
        idf8 = alloc(es, nc, "idf8", [8, 8])
        mk.v(lambda E: E.tensor_copy(idf8[:], identb[0:8, 0:8]), reads=[identb[0:8, 0:8]], writes=[idf8[:]])
        pt8 = alloc(es, nc, "pt8", [8, 8], psum=True)
        exT = alloc(es, nc, "exT", [8, 8])
        mk.t(lambda E: E.matmul(pt8[:], rkx[:], idf8[:], start=True, stop=True),
             reads=[rkx[:], idf8[:]], writes=[pt8[:]])
        mk.v(lambda E: E.tensor_copy(exT[:], pt8[:]), reads=[pt8[:]], writes=[exT[:]])
        chst = alloc(es, nc, "chst", [1, 1])
        mk.dma("sync", chst[:], chid_d)
        ch128 = alloc(es, nc, "ch128", [128, 1])
        u_bcast_row(mk, chst[:], ch128[:], onesf[:], ph[:], 1)
        iota8 = alloc(es, nc, "iota8_s", [128, 8])
        mk.dma("sync", iota8[:], iota8r_d)
        eidc = alloc(es, nc, "eidc", [8, 1])
        mk.dma("sync", eidc[:], eids_d)
        selc = alloc(es, nc, "selc", [8, 1])
        mk.v(lambda E: E.tensor_tensor(selc[:], eidc[:], ch128[0:8, :], ALU.is_equal),
             reads=[eidc[:], ch128[0:8, :]], writes=[selc[:]])
        exrow_ps = ph[0:1, 0:8]
        mk.t(lambda E: E.matmul(exrow_ps, selc[:], exT[:], start=True, stop=True),
             reads=[selc[:], exT[:]], writes=[exrow_ps])
        exrow = alloc(es, nc, "exrow", [1, 8])
        mk.v(lambda E: E.tensor_copy(exrow[:], exrow_ps), reads=[exrow_ps], writes=[exrow[:]])
        exb = alloc(es, nc, "exb", [128, 8])
        u_bcast_row(mk, exrow[:], exb[:], onesf[:], ph[:], 8)

        idxt = alloc(es, nc, "idxt", [128, 1])
        post = alloc(es, nc, "post", [128, 1])
        pmt = alloc(es, nc, "pmt", [128, 1])
        eqx = alloc(es, nc, "eqx", [128, 8])
        fm = alloc(es, nc, "fm", [128, 4])
        slot = alloc(es, nc, "slot", [128, 1])
        slot_i = alloc(es, nc, "slot_i", [128, 1], I32)
        xct = alloc(es, nc, "xct", [128, 1024])
        xnt = alloc(es, nc, "xnt", [128, 1024], BF16)
        eg = alloc(es, nc, "eg", [128, 1024])
        moe = alloc(es, nc, "moe", [128, 1024])
        for t in range(16):
            r0 = t * 128
            mk.dma("sync", idxt[:], idx_d[r0:r0 + 128, :], chan=1)
            mk.dma("sync", post[:], pos_d[r0:r0 + 128, :], chan=1)
            mk.dma("sync", pmt[:], pmax_d[r0:r0 + 128, :], chan=1)
            mk.dma("sync", xct[:], xc_d[r0:r0 + 128, :], chan=1)
            mk.dma("sync", xnt[:], xn_d[r0:r0 + 128, :], chan=1)
            mk.v(lambda E: E.tensor_scalar(eqx[:], iota8[:], idxt[:], None, ALU.is_equal),
                 reads=[iota8[:], idxt[:]], writes=[eqx[:]])
            mk.v(lambda E: E.tensor_mul(eqx[:], eqx[:], exb[:]), reads=[eqx[:], exb[:]], writes=[eqx[:]])
            mk.v(lambda E: E.reduce_sum(fm[:, 0:1], eqx[:], AX.X), reads=[eqx[:]], writes=[fm[:, 0:1]])
            mk.v(lambda E: E.tensor_add(post[:], post[:], fm[:, 0:1]), reads=[post[:], fm[:, 0:1]], writes=[post[:]])
            mk.v(lambda E: E.tensor_single_scalar(fm[:, 1:2], post[:], 2560.0, ALU.is_lt),
                 reads=[post[:]], writes=[fm[:, 1:2]])
            mk.v(lambda E: E.tensor_single_scalar(slot[:], post[:], 2559.0, ALU.min),
                 reads=[post[:]], writes=[slot[:]])
            mk.v(lambda E: E.scalar_tensor_tensor(slot[:], idxt[:], 2560.0, slot[:], ALU.mult, ALU.add),
                 reads=[idxt[:], slot[:]], writes=[slot[:]])
            mk.v(lambda E: E.tensor_single_scalar(slot[:], slot[:], -20480.0, ALU.add),
                 reads=[slot[:]], writes=[slot[:]])
            mk.v(lambda E: E.tensor_scalar(slot[:], slot[:], fm[:, 1:2], 20480.0, ALU.mult, ALU.add),
                 reads=[slot[:], fm[:, 1:2]], writes=[slot[:]])
            mk.v(lambda E: E.tensor_copy(slot_i[:], slot[:]), reads=[slot[:]], writes=[slot_i[:]])
            from concourse.bass_core_rust import IndirectOffsetOnAxis as IOA
            mk.op("gpsimd", lambda E: E.indirect_dma_start(
                    eg[:], None, eout_d, IOA(slot_i[:], 0),
                    bounds_check=20481, oob_is_err=False),
                  reads=[eout_d, slot_i[:]], writes=[eg[:]], dma=True, chan=2)
            mk.v(lambda E: E.tensor_mul(fm[:, 2:3], fm[:, 1:2], pmt[:]), reads=[fm[:, 1:2], pmt[:]],
                 writes=[fm[:, 2:3]])
            mk.v(lambda E: E.tensor_single_scalar(fm[:, 3:4], fm[:, 1:2], 1.0, ALU.subtract),
                 reads=[fm[:, 1:2]], writes=[fm[:, 3:4]])
            mk.v(lambda E: E.tensor_single_scalar(fm[:, 3:4], fm[:, 3:4], -1.0, ALU.mult),
                 reads=[fm[:, 3:4]], writes=[fm[:, 3:4]])
            mk.v(lambda E: E.tensor_scalar_mul(eg[:], eg[:], fm[:, 2:3]), reads=[eg[:], fm[:, 2:3]], writes=[eg[:]])
            mk.v(lambda E: E.scalar_tensor_tensor(moe[:], xnt[:], fm[:, 3:4], eg[:], ALU.mult, ALU.add),
                 reads=[xnt[:], fm[:, 3:4], eg[:]], writes=[moe[:]])
            mk.v(lambda E: E.tensor_add(moe[:], moe[:], xct[:]), reads=[moe[:], xct[:]], writes=[moe[:]])
            mk.dma("gpsimd", xo_d[r0:r0 + 128, :], moe[:])
        mk.emit()
    return nc



def build_mla():
    nc = bass.Bass("TRN2", target_bir_lowering=False)
    x_d = nc.dram_tensor("x", [4096, 1024], F32, kind="ExternalInput").ap()
    wq_d = nc.dram_tensor("wq", [1024, 512], F32, kind="ExternalInput").ap()
    wqr_d = nc.dram_tensor("wqr", [1024, 256], F32, kind="ExternalInput").ap()
    wkr_d = nc.dram_tensor("wkr", [1024, 32], F32, kind="ExternalInput").ap()
    wkvd_d = nc.dram_tensor("wkvd", [1024, 256], F32, kind="ExternalInput").ap()
    wku_d = nc.dram_tensor("wku", [256, 512], F32, kind="ExternalInput").ap()
    wvu_d = nc.dram_tensor("wvu", [256, 512], F32, kind="ExternalInput").ap()
    wo_d = nc.dram_tensor("wo", [512, 1024], F32, kind="ExternalInput").ap()
    cosq_d = nc.dram_tensor("cosq", [4096, 128], F32, kind="ExternalInput").ap()
    sinq_d = nc.dram_tensor("sinq", [4096, 128], F32, kind="ExternalInput").ap()
    cosk_d = nc.dram_tensor("cosk", [4096, 16], F32, kind="ExternalInput").ap()
    sink_d = nc.dram_tensor("sink", [4096, 16], F32, kind="ExternalInput").ap()
    tri_d = nc.dram_tensor("tri", [128, 128], F32, kind="ExternalInput").ap()
    lobo_d = nc.dram_tensor("lobo", [1, 8], F32, kind="ExternalInput").ap()
    identb_d, onesf_d = std_consts(nc)
    po_d = nc.dram_tensor("po", [4096, 1024], F32, kind="ExternalOutput").ap()
    ctx_sc = nc.dram_tensor("ctx_sc", [512, 4096], BF16, kind="Internal").ap()

    SC = 1.0 / math.sqrt(96.0)
    mk = MK(nc)
    es = contextlib.ExitStack()
    with es:
        identb, onesf = load_consts(mk, es, nc, identb_d, onesf_d)
        eps128 = alloc(es, nc, "eps128", [128, 1])
        mk.v(lambda E: E.memset(eps128[:], 1e-5), writes=[eps128[:]])
        S = alloc(es, nc, "S", [128, 4096])
        wstage = S
        PT = alloc(es, nc, "PT", [128, 16384], BF16)
        wq_sb = PT[:, 0:4096]
        wqr_sb = PT[:, 4096:6144]
        wkr_sb = PT[:, 6144:6400]
        wkvd_sb = PT[:, 6400:8448]
        wku_sb = PT[:, 8448:9472]
        wvu_sb = PT[:, 9472:10496]
        for (wd, wsb, ncols, nk) in ((wq_d, wq_sb, 512, 8), (wqr_d, wqr_sb, 256, 8),
                                     (wkr_d, wkr_sb, 32, 8), (wkvd_d, wkvd_sb, 256, 8),
                                     (wku_d, wku_sb, 512, 2), (wvu_d, wvu_sb, 512, 2)):
            for k in range(nk):
                st = wstage[:, 0:ncols]
                mk.dma("sync", st, wd[k * 128:(k + 1) * 128, :])
                dst = wsb[:, k * ncols:(k + 1) * ncols]
                mk.v(lambda E, st=st, dst=dst: E.tensor_copy(dst, st), reads=[st], writes=[dst])
        wo4 = [alloc(es, nc, f"wo4_{j}", [128, 1024], BF16) for j in range(4)]
        for j in range(4):
            st = wstage[:, 0:1024]
            mk.dma("sync", st, wo_d[j * 128:(j + 1) * 128, :])
            dst = wo4[j][:]
            mk.v(lambda E, st=st, dst=dst: E.tensor_copy(dst, st), reads=[st], writes=[dst])
        tri = alloc(es, nc, "tri_sb", [128, 128])
        mk.dma("sync", tri[:], tri_d)
        lobr = alloc(es, nc, "lobr", [1, 8])
        mk.dma("sync", lobr[:], lobo_d)
        mk.s(lambda E: E.activation(out=lobr[:], in_=lobr[:], func=AF.Exp), reads=[lobr[:]], writes=[lobr[:]])
        lob128 = alloc(es, nc, "lob128", [128, 8])
        ph = alloc(es, nc, "ph", [128, 512], psum=True)
        u_bcast_row(mk, lobr[:], lob128[:], onesf[:], ph[:], 8)

        xtile = S[:, 2048:3072]
        P = alloc(es, nc, "P", [128, 4096], BF16)
        normed = alloc(es, nc, "normed", [128, 1024], BF16)
        stats = alloc(es, nc, "stats", [128, 8])
        xf = alloc(es, nc, "xf", [128, 1024], BF16)
        latb = alloc(es, nc, "latb", [128, 256], BF16)
        latfT = alloc(es, nc, "latfT", [128, 256], BF16)
        qr32 = S[:, 3072:3328]
        kr32 = alloc(es, nc, "kr32", [128, 32])
        ta = S[:, 3328:3456]
        tb = S[:, 3456:3584]
        qro = alloc(es, nc, "qro", [128, 256], BF16)
        kro = alloc(es, nc, "kro", [128, 32], BF16)
        qcb = P[:, 0:512]
        cq = S[:, 3584:3712]
        sq = S[:, 3712:3840]
        ck = S[:, 3840:3856]
        sk = S[:, 3856:3872]
        QcT = [alloc(es, nc, f"QcT{j}", [128, 4096], BF16) for j in range(4)]
        QrT = [alloc(es, nc, f"QrT{j}", [128, 4096], BF16) for j in range(3)]
        KcT = [alloc(es, nc, f"KcT{j}", [128, 4096], BF16) for j in range(4)]
        KrT3 = alloc(es, nc, "KrT3", [96, 4096], BF16)
        vt = alloc(es, nc, "vt", [128, 32 * 512], BF16)
        ctfb = alloc(es, nc, "ctfb", [128, 512], BF16)
        ctevac = alloc(es, nc, "ctevac", [64, 512], BF16)
        outsb = alloc(es, nc, "outsb", [128, 1024])
        ph2c = alloc(es, nc, "ph2c", [128, 512], psum=True)
        psT = alloc(es, nc, "psT", [128, 1024], BF16, psum=True)
        psc = alloc(es, nc, "psc", [128, 1024], psum=True)
        psc2 = alloc(es, nc, "psc2", [128, 1024], psum=True)
        pctx = alloc(es, nc, "pctx", [64, 512], psum=True)

        for i in range(32):
            r0 = i * 128
            mk.dma("sync", xtile, x_d[r0:r0 + 128, :], chan=1)
            u_ln(mk, xtile, normed[:], stats[:, 0:4], 1024, eps_ap=eps128[:], sq_ap=S[:, 0:1024])
            u_transpose(mk, lambda j: normed[:, j * 128:(j + 1) * 128],
                        lambda j: xf[:, j * 128:(j + 1) * 128], 8, identb[:], psT)
            u_mm(mk, ph[:, 0:256],
                 [xf[:, k * 128:(k + 1) * 128] for k in range(8)],
                 [wkvd_sb[:, k * 256:(k + 1) * 256] for k in range(8)])
            mk.v(lambda E: E.tensor_copy(latb[:], ph[:, 0:256]), reads=[ph[:, 0:256]], writes=[latb[:]])
            for j in range(2):
                src = latb[:][:, j * 128:(j + 1) * 128]
                mk.t(lambda E, src=src: E.transpose(psT[:, 0:128], src, identb[:]),
                     reads=[src, identb[:]], writes=[psT[:, 0:128]])
                dst = latfT[:, j * 128:(j + 1) * 128]
                mk.v(lambda E, dst=dst: E.tensor_copy(dst, psT[:, 0:128]),
                     reads=[psT[:, 0:128]], writes=[dst])
            u_mm(mk, ph[:, 0:512],
                 [latfT[:, k * 128:(k + 1) * 128] for k in range(2)],
                 [wvu_sb[:, k * 512:(k + 1) * 512] for k in range(2)])
            vdst = vt[:, i * 512:(i + 1) * 512]
            mk.s(lambda E, vdst=vdst: E.activation(out=vdst, in_=ph[:, 0:512], func=AF.Copy),
                 reads=[ph[:, 0:512]], writes=[vdst])
            u_mm(mk, ph[:, 0:512],
                 [latfT[:, k * 128:(k + 1) * 128] for k in range(2)],
                 [wku_sb[:, k * 512:(k + 1) * 512] for k in range(2)])
            mk.v(lambda E: E.tensor_copy(qcb, ph[:, 0:512]), reads=[ph[:, 0:512]], writes=[qcb])
            for j in range(4):
                src = qcb[:, j * 128:(j + 1) * 128]
                mk.t(lambda E, src=src: E.transpose(psT[:, 0:128], src, identb[:]),
                     reads=[src, identb[:]], writes=[psT[:, 0:128]])
                for half in range(2):
                    h = 2 * j + half
                    tns, rb = h % 4, 64 * (h // 4)
                    dst = KcT[tns][rb:rb + 64, i * 128:(i + 1) * 128]
                    srcp = psT[64 * half:64 * half + 64, 0:128]
                    mk.v(lambda E, dst=dst, srcp=srcp: E.tensor_copy(dst, srcp),
                         reads=[srcp], writes=[dst])
            u_mm(mk, ph[:, 0:512],
                 [xf[:, k * 128:(k + 1) * 128] for k in range(8)],
                 [wq_sb[:, k * 512:(k + 1) * 512] for k in range(8)])
            mk.v(lambda E: E.tensor_copy(qcb, ph[:, 0:512]), reads=[ph[:, 0:512]], writes=[qcb])
            for j in range(4):
                src = qcb[:, j * 128:(j + 1) * 128]
                mk.t(lambda E, src=src: E.transpose(psT[:, 0:128], src, identb[:]),
                     reads=[src, identb[:]], writes=[psT[:, 0:128]])
                for half in range(2):
                    h = 2 * j + half
                    tns, rb = h % 4, 64 * (h // 4)
                    dst = QcT[tns][rb:rb + 64, i * 128:(i + 1) * 128]
                    srcp = psT[64 * half:64 * half + 64, 0:128]
                    mk.v(lambda E, dst=dst, srcp=srcp: E.tensor_copy(dst, srcp),
                         reads=[srcp], writes=[dst])
            u_mm(mk, ph[:, 0:256],
                 [xf[:, k * 128:(k + 1) * 128] for k in range(8)],
                 [wqr_sb[:, k * 256:(k + 1) * 256] for k in range(8)])
            mk.v(lambda E: E.tensor_copy(qr32, ph[:, 0:256]), reads=[ph[:, 0:256]], writes=[qr32])
            mk.dma("sync", cq, cosq_d[r0:r0 + 128, :], chan=1)
            mk.dma("sync", sq, sinq_d[r0:r0 + 128, :], chan=1)
            mk.dma("sync", ck, cosk_d[r0:r0 + 128, :], chan=1)
            mk.dma("sync", sk, sink_d[r0:r0 + 128, :], chan=1)
            q3 = qr32.rearrange("p (h c) -> p h c", c=32)
            c3 = cq.rearrange("p (h c) -> p h c", c=16)
            s3 = sq.rearrange("p (h c) -> p h c", c=16)
            ta3 = ta.rearrange("p (h c) -> p h c", c=16)
            tb3 = tb.rearrange("p (h c) -> p h c", c=16)
            o3 = qro[:].rearrange("p (h c) -> p h c", c=32)
            mk.v(lambda E: E.tensor_tensor(ta3, q3[:, :, 0:16], c3, ALU.mult),
                 reads=[qr32, cq], writes=[ta])
            mk.v(lambda E: E.tensor_tensor(tb3, q3[:, :, 16:32], s3, ALU.mult),
                 reads=[qr32, sq], writes=[tb])
            mk.v(lambda E: E.tensor_tensor(o3[:, :, 0:16], ta3, tb3, ALU.subtract),
                 reads=[ta, tb], writes=[qro[:]])
            mk.v(lambda E: E.tensor_tensor(ta3, q3[:, :, 16:32], c3, ALU.mult),
                 reads=[qr32, cq], writes=[ta])
            mk.v(lambda E: E.tensor_tensor(tb3, q3[:, :, 0:16], s3, ALU.mult),
                 reads=[qr32, sq], writes=[tb])
            mk.v(lambda E: E.tensor_tensor(o3[:, :, 16:32], ta3, tb3, ALU.add),
                 reads=[ta, tb], writes=[qro[:]])
            for j in range(2):
                src = qro[:, j * 128:(j + 1) * 128]
                mk.t(lambda E, src=src: E.transpose(psT[:, 0:128], src, identb[:]),
                     reads=[src, identb[:]], writes=[psT[:, 0:128]])
                for hh in range(4):
                    h = 4 * j + hh
                    tns, rb = h // 3, 32 * (h % 3)
                    dst = QrT[tns][rb:rb + 32, i * 128:(i + 1) * 128]
                    srcp = psT[32 * hh:32 * hh + 32, 0:128]
                    mk.v(lambda E, dst=dst, srcp=srcp: E.tensor_copy(dst, srcp),
                         reads=[srcp], writes=[dst])
            u_mm(mk, ph[:, 0:32],
                 [xf[:, k * 128:(k + 1) * 128] for k in range(8)],
                 [wkr_sb[:, k * 32:(k + 1) * 32] for k in range(8)])
            mk.v(lambda E: E.tensor_copy(kr32[:], ph[:, 0:32]), reads=[ph[:, 0:32]], writes=[kr32[:]])
            mk.v(lambda E: E.tensor_tensor(ta[:, 0:16], kr32[:, 0:16], ck, ALU.mult),
                 reads=[kr32[:], ck], writes=[ta[:, 0:16]])
            mk.v(lambda E: E.tensor_tensor(tb[:, 0:16], kr32[:, 16:32], sk, ALU.mult),
                 reads=[kr32[:], sk], writes=[tb[:, 0:16]])
            mk.v(lambda E: E.tensor_tensor(kro[:, 0:16], ta[:, 0:16], tb[:, 0:16], ALU.subtract),
                 reads=[ta[:, 0:16], tb[:, 0:16]], writes=[kro[:]])
            mk.v(lambda E: E.tensor_tensor(ta[:, 0:16], kr32[:, 16:32], ck, ALU.mult),
                 reads=[kr32[:], ck], writes=[ta[:, 0:16]])
            mk.v(lambda E: E.tensor_tensor(tb[:, 0:16], kr32[:, 0:16], sk, ALU.mult),
                 reads=[kr32[:], sk], writes=[tb[:, 0:16]])
            mk.v(lambda E: E.tensor_tensor(kro[:, 16:32], ta[:, 0:16], tb[:, 0:16], ALU.add),
                 reads=[ta[:, 0:16], tb[:, 0:16]], writes=[kro[:]])
            mk.t(lambda E: E.transpose(psT[0:32, 0:128], kro[:, 0:32], identb[:]),
                 reads=[kro[:], identb[:]], writes=[psT[0:32, 0:128]])
            for cc in range(3):
                dst = KrT3[32 * cc:32 * cc + 32, i * 128:(i + 1) * 128]
                mk.v(lambda E, dst=dst: E.tensor_copy(dst, psT[0:32, 0:128]),
                     reads=[psT[0:32, 0:128]], writes=[dst])
        for h in range(8):
            qb = 64 * (h // 4)
            rb = 32 * (h % 3)
            for qc in range(8):
                KN = 4 * qc + 4
                for qs in range(4):
                    qt = 4 * qc + qs
                    KEXT = qt + 1
                    q0 = qt * 128
                    lhsc = QcT[h % 4][qb:qb + 64, q0:q0 + 128]
                    lhsr = QrT[h // 3][rb:rb + 32, q0:q0 + 128]
                    for p0 in range(0, KEXT * 128, 1024):
                        p1 = min(KEXT * 128, p0 + 1024)
                        for c0 in range(p0, p1, 512):
                            c1 = min(p1, c0 + 512)
                            rhsc = KcT[h % 4][qb:qb + 64, c0:c1]
                            rhsr = KrT3[rb:rb + 32, c0:c1]
                            d1 = psc[:, c0 - p0:c1 - p0]
                            d2 = psc2[:, c0 - p0:c1 - p0]
                            mk.t(lambda E, d1=d1, lhsc=lhsc, rhsc=rhsc: E.matmul(d1, lhsc, rhsc,
                                                                                start=True, stop=True),
                                 reads=[lhsc, rhsc], writes=[d1])
                            mk.t(lambda E, d2=d2, lhsr=lhsr, rhsr=rhsr: E.matmul(d2, lhsr, rhsr,
                                                                                start=True, stop=True),
                                 reads=[lhsr, rhsr], writes=[d2])
                        sap = S[:, p0:p1]
                        pap = psc[:, 0:p1 - p0]
                        pap2 = psc2[:, 0:p1 - p0]
                        mk.s(lambda E, sap=sap, pap=pap: E.activation(out=sap, in_=pap, func=AF.Copy,
                                                                     scale=SC),
                             reads=[pap], writes=[sap])
                        mk.v(lambda E, sap=sap, pap2=pap2: E.scalar_tensor_tensor(sap, pap2, SC, sap,
                                                                                  ALU.mult, ALU.add),
                             reads=[pap2, sap], writes=[sap])
                    dap = S[:, (KEXT - 1) * 128:KEXT * 128]
                    mk.v(lambda E, dap=dap: E.tensor_add(dap, dap, tri[:]),
                         reads=[dap, tri[:]], writes=[dap])
                    sall = S[:, 0:KEXT * 128]
                    mk.v(lambda E, sall=sall: E.reduce_max(stats[:, 0:1], sall, AX.X),
                         reads=[sall], writes=[stats[:, 0:1]])
                    mk.v(lambda E: E.tensor_single_scalar(stats[:, 0:1], stats[:, 0:1], -1.0, ALU.mult),
                         reads=[stats[:, 0:1]], writes=[stats[:, 0:1]])
                    pall = P[:, 0:KEXT * 128]
                    mk.s(lambda E, sall=sall, pall=pall: E.activation(out=pall, in_=sall, func=AF.Exp,
                                                                     bias=stats[:, 0:1],
                                                                     accum_out=stats[:, 1:2]),
                         reads=[sall, stats[:, 0:1]], writes=[pall, stats[:, 1:2]])
                    cap_h = lob128[:, h:h + 1]
                    mk.v(lambda E, cap_h=cap_h: E.tensor_tensor(stats[:, 1:2], stats[:, 1:2], cap_h,
                                                                ALU.add),
                         reads=[stats[:, 1:2], cap_h], writes=[stats[:, 1:2]])
                    mk.v(lambda E: E.reciprocal(stats[:, 2:3], stats[:, 1:2]),
                         reads=[stats[:, 1:2]], writes=[stats[:, 2:3]])
                    mk.v(lambda E, pall=pall: E.tensor_scalar_mul(pall, pall, stats[:, 2:3]),
                         reads=[pall, stats[:, 2:3]], writes=[pall])
                    for kt0 in range(0, KEXT, 8):
                        kt1 = min(KEXT, kt0 + 8)
                        for kt in range(kt0, kt1):
                            src = P[:, kt * 128:(kt + 1) * 128]
                            slot = kt - kt0
                            mk.t(lambda E, src=src, slot=slot: E.transpose(
                                    psT[:, slot * 128:(slot + 1) * 128], src, identb[:]),
                                 reads=[src, identb[:]], writes=[psT[:, slot * 128:(slot + 1) * 128]])
                        nsl = kt1 - kt0
                        src3 = psT[:, 0:nsl * 128].rearrange("p (k c) -> p k c", c=128)
                        dst3 = PT[:].rearrange("p (k q c) -> p k q c", q=4, c=128)[:, kt0:kt1, qs, :]
                        mk.v(lambda E, src3=src3, dst3=dst3: E.tensor_copy(dst3, src3),
                             reads=[psT[:, 0:nsl * 128]],
                             writes=[PT[:, kt0 * 512:(kt1 - 1) * 512 + (qs + 1) * 128]])
                for kt in range(4 * qc + 1, KN):
                    nz = (kt - 4 * qc) * 128
                    zap = PT[:, kt * 512:kt * 512 + nz]
                    mk.v(lambda E, zap=zap: E.memset(zap, 0.0), writes=[zap])
                for kt in range(KN):
                    lhsv = vt[:, kt * 512 + h * 64:kt * 512 + h * 64 + 64]
                    rhsp = PT[:, kt * 512:(kt + 1) * 512]
                    mk.t(lambda E, lhsv=lhsv, rhsp=rhsp, kt=kt, KN=KN: E.matmul(
                            pctx[:, 0:512], lhsv, rhsp, start=(kt == 0), stop=(kt == KN - 1)),
                         reads=[lhsv, rhsp], writes=[pctx[:, 0:512]])
                mk.v(lambda E: E.tensor_copy(ctevac[:], pctx[:, 0:512]),
                     reads=[pctx[:, 0:512]], writes=[ctevac[:]])
                mk.dma("gpsimd", ctx_sc[h * 64:(h + 1) * 64, qc * 512:(qc + 1) * 512], ctevac[:])
        for i in range(32):
            for j in range(4):
                mk.dma("sync", ctfb[0:64, j * 128:(j + 1) * 128],
                       ctx_sc[(2 * j) * 64:(2 * j) * 64 + 64, i * 128:(i + 1) * 128], chan=1)
                mk.dma("sync", ctfb[64:128, j * 128:(j + 1) * 128],
                       ctx_sc[(2 * j + 1) * 64:(2 * j + 1) * 64 + 64, i * 128:(i + 1) * 128], chan=1)
            for ch in range(2):
                u_mm(mk, ph[:, 0:512],
                     [ctfb[0:64, j * 128:(j + 1) * 128] for j in range(4)],
                     [wo4[j][0:64, ch * 512:(ch + 1) * 512] for j in range(4)])
                oap = outsb[:, ch * 512:(ch + 1) * 512]
                mk.v(lambda E, oap=oap: E.tensor_copy(oap, ph[:, 0:512]),
                     reads=[ph[:, 0:512]], writes=[oap])
                u_mm(mk, ph2c[:, 0:512],
                     [ctfb[64:128, j * 128:(j + 1) * 128] for j in range(4)],
                     [wo4[j][64:128, ch * 512:(ch + 1) * 512] for j in range(4)])
                mk.v(lambda E, oap=oap: E.tensor_add(oap, oap, ph2c[:, 0:512]),
                     reads=[oap, ph2c[:, 0:512]], writes=[oap])
            mk.dma("gpsimd", po_d[i * 128:(i + 1) * 128, :], outsb[:])
        mk.emit()
    return nc


def rope_tables_mla():
    inv = (1.0 / (10000.0 ** (np.arange(16, dtype=np.float32) / 16))).astype(np.float32)
    pos = np.arange(4096, dtype=np.float32)
    ang = (pos[:, None] * inv[None, :]).astype(np.float32)
    return np.cos(ang).astype(np.float32), np.sin(ang).astype(np.float32)


def mla_inmaps(x_full, wq, wqr, wkr, wkvd, wku, wvu, wo, lobo, consts):
    cos, sin = rope_tables_mla()
    cosq = np.tile(cos, (1, 8)).copy()
    sinq = np.tile(sin, (1, 8)).copy()
    tri = np.triu(np.full((128, 128), -1e30, np.float32), k=1)
    ins = []
    for c in range(8):
        b, hh = c // 2, c % 2
        m = dict(consts)
        m["x"] = np.ascontiguousarray(x_full[b])
        m["wq"] = np.ascontiguousarray(wq[:, hh * 512:(hh + 1) * 512])
        m["wqr"] = np.ascontiguousarray(wqr[:, hh * 256:(hh + 1) * 256])
        m["wkr"] = wkr
        m["wkvd"] = wkvd
        m["wku"] = np.ascontiguousarray(wku[:, hh * 512:(hh + 1) * 512])
        m["wvu"] = np.ascontiguousarray(wvu[:, hh * 512:(hh + 1) * 512])
        m["wo"] = np.ascontiguousarray(wo[hh * 512:(hh + 1) * 512, :])
        m["cosq"] = cosq; m["sinq"] = sinq; m["cosk"] = cos; m["sink"] = sin
        m["tri"] = tri
        m["lobo"] = np.ascontiguousarray(lobo[hh * 8:(hh + 1) * 8]).reshape(1, 8)
        ins.append(m)
    return ins



def _pairs(po_list):
    p0, p1 = [], []
    for c in range(8):
        b, h = c // 2, c % 2
        r0 = h * 2048
        p0.append(np.ascontiguousarray(np.asarray(po_list[2 * b])[r0:r0 + 2048]))
        p1.append(np.ascontiguousarray(np.asarray(po_list[2 * b + 1])[r0:r0 + 2048]))
    return p0, p1


def _full(res_list, key):
    return np.stack([np.concatenate([np.asarray(res_list[2 * b + h][key]) for h in range(2)], axis=0)
                     for b in range(4)])


def kernel(**inp):
    f32 = np.float32
    g = {k: np.asarray(v, f32) if np.asarray(v).dtype == np.float32 else np.asarray(v) for k, v in inp.items()}
    consts = std_consts_np()
    cores = list(range(8))
    x0 = np.ascontiguousarray(g["L0_attn_a_x"].astype(f32))

    nc = build_quant()
    rq = run_bass_kernel_spmd(nc, quant_inmaps(g["L0_ffn_W_fc"].astype(f32),
                                               g["L0_ffn_W_proj"].astype(f32), consts),
                              core_ids=cores).results
    wfcT = np.concatenate([np.asarray(r["o_fcT"]) for r in rq], axis=1)
    wprojT = np.concatenate([np.asarray(r["o_projT"]) for r in rq], axis=1)

    nc = build_attn_gqa()
    r1 = run_bass_kernel_spmd(nc, attn_gqa_inmaps(
        x0, g["L0_attn_a_Wq"].astype(f32), g["L0_attn_a_Wk"].astype(f32),
        g["L0_attn_a_Wv"].astype(f32), g["L0_attn_a_Wo"].astype(f32), consts),
        core_ids=cores).results
    p0, p1 = _pairs([r["po"] for r in r1])

    nc = build_ffn("first")
    r2 = run_bass_kernel_spmd(nc, ffn_inmaps(x0, p0, p1, None, wfcT, wprojT,
                                             g["L0_ffn_ln_w"].astype(f32), g["L0_ffn_ln_b"].astype(f32),
                                             consts, "first"), core_ids=cores).results
    x2 = _full(r2, "xo")

    nc = build_gau()
    r3 = run_bass_kernel_spmd(nc, gau_inmaps(
        x2, g["L0_attn_b_ln_g"].astype(f32), g["L0_attn_b_ln_b"].astype(f32),
        g["L0_attn_b_W_hidden"].astype(f32), g["L0_attn_b_b_hidden"].astype(f32),
        g["L0_attn_b_W_qk"].astype(f32), g["L0_attn_b_b_qk"].astype(f32),
        g["L0_attn_b_os_gamma"].astype(f32), g["L0_attn_b_os_beta"].astype(f32),
        g["L0_attn_b_W_out"].astype(f32), consts), core_ids=cores).results
    p0, p1 = _pairs([r["po"] for r in r3])

    nc = build_ffn("post_s3")
    r4 = run_bass_kernel_spmd(nc, ffn_inmaps(x2, p0, p1, g["L0_attn_b_b_out"].astype(f32),
                                             wfcT, wprojT, g["L0_ffn_ln_w"].astype(f32),
                                             g["L0_ffn_ln_b"].astype(f32), consts, "post_s3"),
                              core_ids=cores).results
    x4 = _full(r4, "xo")

    nc = build_mamba_a()
    ra = run_bass_kernel_spmd(nc, mamba_a_inmaps(
        x4, g["L1_attn_a_in_w"].astype(f32), g["L1_attn_a_conv_w"].astype(f32),
        g["L1_attn_a_conv_b"].astype(f32), g["L1_attn_a_xp_w"].astype(f32),
        g["L1_attn_a_dt_w"].astype(f32), g["L1_attn_a_dt_b"].astype(f32), consts),
        core_ids=cores).results
    alog = g["L1_attn_a_A_log"].astype(f32)
    ins_b = []
    for c in range(8):
        b, dd = c // 2, c % 2
        dtT = np.concatenate([np.asarray(ra[2 * b + h][f"dtT{dd}"]) for h in range(2)], axis=1)
        dtx = np.concatenate([np.asarray(ra[2 * b + h][f"dtxT{dd}"]) for h in range(2)], axis=1)
        bt = np.concatenate([np.asarray(ra[2 * b + h][f"bt{dd}"]) for h in range(2)], axis=1)
        ct = np.concatenate([np.asarray(ra[2 * b + h][f"ct{dd}"]) for h in range(2)], axis=1)
        if dd == 1:
            dtT = dtT[:, ::-1].copy(); dtx = dtx[:, ::-1].copy()
            bt = bt[:, ::-1].copy(); ct = ct[:, ::-1].copy()
        m = dict(consts)
        m["dtT"] = dtT; m["dtx"] = dtx; m["bt"] = bt; m["ct"] = ct
        m["alog"] = np.ascontiguousarray(alog[dd])
        ins_b.append(m)
    nc = build_mamba_b()
    rb = run_bass_kernel_spmd(nc, ins_b, core_ids=cores).results
    yT = {}
    for c in range(8):
        b, dd = c // 2, c % 2
        y = np.asarray(rb[c]["yT"])
        if dd == 1:
            y = y[:, ::-1].copy()
        yT[(b, dd)] = y
    owT = np.ascontiguousarray(g["L1_attn_a_out_w"].astype(f32).transpose(0, 2, 1))
    projw = g["L1_attn_a_proj_w"].astype(f32)
    Dsk = g["L1_attn_a_D"].astype(f32)
    dcols = np.zeros((128, 32), f32)
    for dd in range(2):
        for dt in range(16):
            dcols[:, dd * 16 + dt] = Dsk[dd, dt * 128:(dt + 1) * 128]
    ins_c = []
    for c in range(8):
        b, h = c // 2, c % 2
        r0 = h * 2048
        m = dict(consts)
        m["x3"] = np.ascontiguousarray(x4[b, r0:r0 + 2048])
        for dd in range(2):
            m[f"yTs{dd}"] = np.ascontiguousarray(yT[(b, dd)][:, r0:r0 + 2048])
            m[f"xcTs{dd}"] = np.asarray(ra[c][f"xcT{dd}"])
            m[f"szTs{dd}"] = np.asarray(ra[c][f"szT{dd}"])
            m[f"owT{dd}"] = np.ascontiguousarray(owT[dd])
            m[f"pw{dd}"] = np.ascontiguousarray(projw[dd * 1024:(dd + 1) * 1024])
        m["dcols"] = dcols
        m["pb"] = g["L1_attn_a_proj_b"].astype(f32).reshape(1, 1024)
        ins_c.append(m)
    nc = build_mamba_c()
    rc = run_bass_kernel_spmd(nc, ins_c, core_ids=cores).results
    x5 = _full(rc, "xo")

    iota8r = np.tile(np.arange(8, dtype=f32), (128, 1))
    eids = np.arange(8, dtype=f32).reshape(8, 1)
    tokid = np.tile(np.arange(16384, dtype=f32).reshape(16384, 1), (1, 8))
    wr = g["L1_ffn_Wr"].astype(f32)
    w1 = g["L1_ffn_W1"].astype(f32)
    w2 = g["L1_ffn_W2"].astype(f32)

    def run_moe(x_base, plist):
        has_p = plist is not None
        nc_r = build_moe_route(has_p)
        ins = []
        for c in range(8):
            b, h = c // 2, c % 2
            m = dict(consts)
            m["xb"] = np.ascontiguousarray(x_base[b, h * 2048:(h + 1) * 2048])
            if has_p:
                m["p0"] = plist[0][c]; m["p1"] = plist[1][c]
            m["wr"] = wr; m["iota8r"] = iota8r; m["eids"] = eids
            ins.append(m)
        rr = run_bass_kernel_spmd(nc_r, ins, core_ids=cores).results
        xn_full = np.concatenate([np.asarray(r["xn_o"]) for r in rr], axis=0)
        idx_all = np.concatenate([np.asarray(r["idx_o"]) for r in rr], axis=0)
        pos_all = np.concatenate([np.asarray(r["pos_o"]).reshape(2048, 1) for r in rr], axis=0)
        cntT = np.concatenate([np.asarray(r["cnt_o"]) for r in rr], axis=1)
        nc_e = build_moe_expert()
        ins_e = []
        for e in range(8):
            m = dict(consts)
            m["xn"] = xn_full; m["idx"] = idx_all; m["pos"] = pos_all
            m["cntT"] = np.ascontiguousarray(cntT)
            m["w1"] = np.ascontiguousarray(w1[e]); m["w2"] = np.ascontiguousarray(w2[e])
            m["eid"] = np.zeros((1, 1), f32) + e
            m["iota8r"] = iota8r; m["tokid"] = tokid
            ins_e.append(m)
        re = run_bass_kernel_spmd(nc_e, ins_e, core_ids=cores).results
        eout_ext = np.concatenate([np.concatenate([np.asarray(r["eout"]) for r in re], axis=0),
                                   np.zeros((1, 1024), f32)], axis=0)
        nc_c = build_moe_combine()
        ins_c2 = []
        for c in range(8):
            m = dict(consts)
            m["xc"] = np.asarray(rr[c]["xc_o"]); m["xn"] = np.asarray(rr[c]["xn_o"])
            m["eoutx"] = eout_ext
            m["idx"] = np.asarray(rr[c]["idx_o"]); m["pos"] = np.asarray(rr[c]["pos_o"]).reshape(2048, 1)
            m["pmax"] = np.asarray(rr[c]["pmax_o"])
            m["cntT"] = np.ascontiguousarray(cntT)
            m["chid"] = np.zeros((1, 1), f32) + c
            m["iota8r"] = iota8r; m["eids"] = eids
            ins_c2.append(m)
        rc2 = run_bass_kernel_spmd(nc_c, ins_c2, core_ids=cores).results
        return _full(rc2, "xo")

    x6 = run_moe(x5, None)

    nc = build_mla()
    r7 = run_bass_kernel_spmd(nc, mla_inmaps(
        x6, g["L1_attn_b_Wq"].astype(f32), g["L1_attn_b_Wqr"].astype(f32),
        g["L1_attn_b_Wkr"].astype(f32), g["L1_attn_b_Wkvd"].astype(f32),
        g["L1_attn_b_Wku"].astype(f32), g["L1_attn_b_Wvu"].astype(f32),
        g["L1_attn_b_Wo"].astype(f32), g["L1_attn_b_lobo_log"].astype(f32), consts),
        core_ids=cores).results
    p0, p1 = _pairs([r["po"] for r in r7])

    x8 = run_moe(x6, (p0, p1))
    return x8.astype(np.float32)
